# version 28
# baseline (speedup 1.0000x reference)
import numpy as np
import ml_dtypes
import concourse.bass as bass
import concourse.mybir as mybir
from concourse.bass_utils import run_bass_kernel_spmd

F32 = mybir.dt.float32
BF16 = mybir.dt.bfloat16
AF = mybir.ActivationFunctionType
ALU = mybir.AluOpType
AX = mybir.AxisListType

NCORE = 8
D = 1024
L = 4096
NEG = -1.0e30
EPS = 1e-5

ENGINES = ("tensor", "vector", "scalar", "gpsimd", "sync")
SELF_SYNC = {"vector", "scalar", "gpsimd", "sync"}


class Prog:

    def __init__(self, nc):
        self.nc = nc
        self.items = []
        self.cseq = {e: 0 for e in ENGINES}
        self.dseq = {e: 0 for e in ENGINES}
        self.last_w = {}
        self.readers = {}

    def op(self, eng, emit, r=(), w=(), dma=False, tag=""):
        deps = list(self._take_fence(eng))
        for res in r:
            p = self.last_w.get(res)
            if p is not None:
                deps.append(p)
        for res in w:
            p = self.last_w.get(res)
            if p is not None:
                deps.append(p)
            deps.extend(self.readers.get(res, ()))
        if dma:
            self.dseq[eng] += 1
            point = ("d", eng, self.dseq[eng])
        else:
            self.cseq[eng] += 1
            point = ("c", eng, self.cseq[eng])
        for res in w:
            self.last_w[res] = point
            self.readers[res] = []
        for res in r:
            self.readers.setdefault(res, []).append(point)
        best = {}
        for (k, e, s) in deps:
            if best.get((k, e), -1) < s:
                best[(k, e)] = s
        deps = [(k, e, s) for (k, e), s in best.items()]
        self.items.append((eng, emit, deps, point, dma, tag))
        return point

    def fence(self):
        pts = []
        for e in ENGINES:
            if self.cseq[e]:
                pts.append(("c", e, self.cseq[e]))
            if self.dseq[e]:
                pts.append(("d", e, self.dseq[e]))
        self._pending_fence = {e: list(pts) for e in ENGINES}

    def _take_fence(self, eng):
        pf = getattr(self, "_pending_fence", None)
        if pf and pf.get(eng):
            deps = pf[eng]
            pf[eng] = []
            return deps
        return []

    def emit(self):
        nc = self.nc
        csem = {}
        dsem = {}
        for e in ENGINES:
            if self.cseq[e] > 0:
                csem[e] = nc.semaphore(f"c_{e}").__enter__()
            if self.dseq[e] > 0:
                dsem[e] = nc.semaphore(f"d_{e}").__enter__()
        plans = {e: [] for e in ENGINES}
        for (eng, emit, deps, point, dma, tag) in self.items:
            waits = []
            for (k, pe, s2) in deps:
                if k == "c" and pe == eng and eng not in SELF_SYNC:
                    continue
                waits.append((k, pe, s2))
            if dma and point[2] > 1:
                waits.append(("d", eng, point[2] - 1))
            plans[eng].append((waits, emit, point, dma, tag))

        runend = {e: {} for e in ENGINES}
        for e in ENGINES:
            cur_run = []
            for (waits, _em, point, dma, _t) in plans[e]:
                if waits and cur_run:
                    for s2 in cur_run:
                        runend[e][s2] = cur_run[-1]
                    cur_run = []
                if dma:
                    cur_run.append(point[2])
            for s2 in cur_run:
                runend[e][s2] = cur_run[-1]

        water = {e: {} for e in ENGINES}
        final = {e: [] for e in ENGINES}
        for e in ENGINES:
            for (waits, emitf, point, dma, tag) in plans[e]:
                rw = []
                for (k, pe, s2) in waits:
                    if k == "c":
                        need, sem = s2, csem[pe]
                    else:
                        need, sem = 16 * runend[pe].get(s2, s2), dsem[pe]
                    if water[e].get((k, pe), 0) >= need:
                        continue
                    water[e][(k, pe)] = need
                    rw.append((sem, need))
                inc = (dsem[e], 16) if dma else (csem[e], 1)
                final[e].append((rw, emitf, inc, tag))

        with nc.Block() as blk:
            def make(eng):
                def run(handle):
                    for rw, emitf, (sem, n), tag in final[eng]:
                        for (wsem, val) in rw:
                            handle.wait_ge(wsem, val)
                        emitf(handle).then_inc(sem, n)
                    if eng in dsem:
                        handle.wait_ge(dsem[eng], 16 * self.dseq[eng])
                return run
            for e in ENGINES:
                if final[e]:
                    getattr(blk, e)(make(e))
        return {"csem": {e: self.cseq[e] for e in ENGINES},
                "dsem": {e: self.dseq[e] for e in ENGINES},
                "n_ops": len(self.items)}


def psb(c0, c1):
    return [f"psbank{i}" for i in range(c0 // 512, (c1 + 511) // 512)]


def build_nc(debug=False):
    nc = bass.Bass("TRN2", target_bir_lowering=False)
    P = Prog(nc)
    op = P.op

    def inp(name, shape, dt=F32):
        return nc.dram_tensor(name, list(shape), dt, kind="ExternalInput").ap()

    xT_h = inp("xT", (D, L))
    wq_h = inp("wq", (D, D)); wk_h = inp("wk", (D, D)); wv_h = inp("wv", (D, D))
    woa_h = inp("woa", (D, D))
    bq_h = inp("bq", (1, D)); bk_h = inp("bk", (1, D))
    bvT_h = inp("bvT", (128, 8)); boT_h = inp("boT", (128, 8))
    identf_h = inp("identf", (128, 128))
    mtri_h = inp("mtri", (128, 128))
    tab_h = {t: inp(f"tab_{t}", (128, 2048)) for t in ("cs", "ss", "ci", "si")}
    ffn_h = {}
    for l in (0, 1):
        ffn_h[l] = dict(
            wqkv=inp(f"wqkv{l}", (D, 1536)), wof=inp(f"wof{l}", (512, D)),
            w1=inp(f"w1_{l}", (D, 5460)), w2=inp(f"w2_{l}", (2730, D)),
            b1a=inp(f"b1a{l}", (128, 22)), b1g=inp(f"b1g{l}", (128, 22)),
            b2T=inp(f"b2T{l}", (128, 8)), lngT=inp(f"lngT{l}", (128, 8)),
            lnbT=inp(f"lnbT{l}", (128, 8)),
            qsrep=inp(f"qsrep{l}", (1, 2048)), ksrep=inp(f"ksrep{l}", (1, 2048)),
        )
    okind = "ExternalOutput" if debug else "Internal"
    xS = {k: nc.dram_tensor(k, [D, L], F32, kind=okind).ap()
          for k in ("xA", "xB", "xC", "xD", "xE")}
    out_h = nc.dram_tensor("out", [D, L], F32, kind="ExternalOutput").ap()

    PS = nc.alloc_psum_tensor("PS", [128, 4096], F32)
    PSH = PS[:].bitcast(BF16)

    sba = nc.alloc_sbuf_tensor
    IDBF = sba("IDBF", [128, 128], BF16)
    MTRI = sba("MTRI", [128, 128], BF16)
    ONESB = sba("ONESB", [1, 128], BF16)
    ONECB = sba("ONECB", [128, 1], BF16)
    ONES64 = sba("ONES64", [65, 128], F32)
    CEPS = sba("CEPS", [1, 1], F32)
    nc.const_aps.register(F32, EPS, CEPS[:])
    ONESF = sba("ONESF", [1, 128], F32)
    EIGHTF = sba("EIGHTF", [1, 128], F32)
    BQ = sba("BQ", [1, D], BF16)
    BK = sba("BK", [1, D], BF16)
    BVT = sba("BVT", [128, 8], F32)
    BOT = sba("BOT", [128, 8], F32)
    SM = {}
    for l in (0, 1):
        SM[l] = dict(
            b1a=sba(f"B1A{l}", [128, 22], F32), b1g=sba(f"B1G{l}", [128, 22], F32),
            b2T=sba(f"B2T{l}", [128, 8], F32), lngT=sba(f"LNG{l}", [128, 8], F32),
            lnbT=sba(f"LNB{l}", [128, 8], F32),
        )
    STATC = sba("STATC", [128, 64], F32)
    NMX = sba("NMX", [128, 16], F32)
    ROW = sba("ROWB", [1, 2048], F32)

    assert nc.sbuf_base < 33 * 1024, f"bump region overflow {nc.sbuf_base}"
    LIMIT = 229345
    R0 = 33 * 1024
    cur = [R0]

    def nxt(name, shape, dt, at=None):
        offi = cur[0] if at is None else at
        t = nc.alloc_sbuf_tensor_at(name, list(shape), dt, offset=offi)
        if at is None:
            nb = shape[1] * (2 if dt == BF16 else 4)
            cur[0] = offi + (nb + 31) // 32 * 32
        return t

    WQ = nxt("WQ", (128, 8 * D), BF16)
    WK = nxt("WK", (128, 8 * D), BF16)
    WV = nxt("WV", (128, 8 * D), BF16)
    WOA = nxt("WOA", (128, 8 * D), BF16)
    XC = [nxt("XCs", (128, 4096), F32)] * 2
    XR = [nxt("XRs", (128, 512), F32)] * 2
    XPT = [nxt("XPT0", (128, 4096), BF16)] * 2
    QV = [nxt(f"QV{i}", (128, 4096), BF16) for i in (0, 1)]
    KV = [nxt(f"KV{i}", (128, 4096), BF16) for i in (0, 1)]
    VVT = [nxt(f"VVT{i}", (128, 2 * 2080), BF16) for i in (0, 1)]
    P2 = [nxt(f"P2_{i}", (128, 1024), BF16) for i in (0, 1)]
    ATOK = nxt("ATOK", (128, 256), BF16)
    ANB = [nxt(f"ANB{i}", (128, 4096), BF16) for i in (0, 1)]
    UNS = nxt("UNS", (128, 4096), BF16)
    RT = nxt("RT", (128, 4096), F32)
    RBT = nxt("RBT", (128, 512), BF16)
    R2T = nxt("R2T", (128, 512), BF16)
    X1T = [nxt("X1Ts", (128, 512), F32)] * 2
    MUB = nxt("MUB", (128, 512), F32)
    RSB = nxt("RSB", (128, 512), F32)
    S1END = cur[0]
    assert S1END < LIMIT, f"S1 region overflow {S1END}"

    cur[0] = R0
    WQKV = nxt("WQKV", (128, 8 * 1536), BF16)
    WOF = nxt("WOF", (128, 4 * 1024), BF16)
    XW = nxt("XW", (128, 8 * 513), F32)
    HT = nxt("HT", (128, 8 * 512), BF16)
    QF = nxt("QF", (128, 2048), F32)
    KF = nxt("KF", (128, 2048), F32)
    QR = nxt("QR", (128, 2048), BF16)
    KR = nxt("KR", (128, 2048), BF16)
    VRA = nxt("VRA", (128, 2048), BF16)
    VRB = nxt("VRB", (128, 2048), BF16)
    qhb_off0 = cur[0]
    QHB = nxt("QHB", (128, 2048), BF16)
    KHB = nxt("KHB", (128, 2048), BF16)
    QT2 = [nxt(f"QT2_{i}", (128, 2048), BF16) for i in (0, 1)]
    KT2 = [nxt(f"KT2_{i}", (128, 2048), BF16) for i in range(3)]
    VRING = [nxt(f"VRING{i}", (128, 4 * 520), BF16) for i in range(3)]
    PB = [nxt(f"PB{i}", (128, 640), BF16) for i in range(4)]
    PTB = [nxt(f"PTB{i}", (128, 640), BF16) for i in range(2)]
    ow_off0 = cur[0]
    OW = nxt("OW", (128, 2048), BF16)
    OT = nxt("OT", (128, 2048), BF16)
    X2T = [nxt(f"X2T{i}", (128, 512), F32) for i in (0, 1)]
    XR2 = [nxt(f"XR2_{i}", (128, 512), F32) for i in (0, 1)]
    INB = [nxt(f"INB{i}", (128, 64), F32) for i in (0, 1)]
    T1Q = nxt("T1Q", (128, 2048), BF16)
    T2Q = nxt("T2Q", (128, 2048), BF16)
    T1K = nxt("T1K", (128, 2048), BF16)
    T2K = nxt("T2K", (128, 2048), BF16)
    TRAW = nxt("TRAW", (128, 2048), F32, at=qhb_off0)
    QSB = nxt("QSB", (128, 2048), F32, at=ow_off0)
    F1END = cur[0]
    assert F1END < LIMIT, f"F1 region overflow {F1END}"

    cur[0] = R0
    W1S = nxt("W1S", (128, 8 * 5460), BF16)
    W2S = nxt("W2S", (128, 22 * 1024), BF16)
    XC2 = [nxt(f"XC2_{i}", (128, 8 * 257), F32) for i in (0, 1)]
    H2B = [nxt(f"H2B_{i}", (128, 8 * 256), BF16) for i in (0, 1)]
    ACTB = nxt("ACTB", (128, 22 * 256), BF16)
    GTMP = nxt("GTMP", (128, 512), F32)
    R2C = nxt("R2C", (128, 256), BF16)
    X3T = [nxt(f"X3T{i}", (128, 256), F32) for i in (0, 1)]
    MUB2 = nxt("MUB2", (128, 256), F32)
    RSB2 = nxt("RSB2", (128, 256), F32)
    F2END = cur[0]
    assert F2END < LIMIT, f"F2 region overflow {F2END}"

    cur[0] = R0
    XF1 = nxt("XF1", (128, 4096), F32)
    XF2 = nxt("XF2", (128, 4096), F32)
    OUTT = [nxt(f"OUTT{i}", (128, 512), F32) for i in (0, 1)]
    FLEND = cur[0]
    assert FLEND < LIMIT

    dbg_bufs = {}

    def dbg(name, ap, deps):
        if not debug:
            return
        dt = nc.dram_tensor("dbg_" + name, list(ap.shape[:2]) if len(ap.shape) == 2
                            else list(ap.shape), ap.dtype, kind="ExternalOutput").ap()
        op("scalar", lambda h, d=dt, s2=ap: h.dma_start(d, s2),
           r=deps, w=[f"dbg{name}"], dma=True, tag="dbg " + name)

    def dma(eng, dst, src, r=(), w=(), tag=""):
        return op(eng, lambda h, d=dst, s=src: h.dma_start(d, s),
                  r=r, w=w, dma=True, tag=tag)

    def dma_cast(dst, src, r=(), w=(), tag=""):
        return op("gpsimd", lambda h, d=dst, s=src: h.dma_start(d, s),
                  r=r, w=w, dma=True, tag=tag)

    dma_cast(IDBF[:], identf_h, w=["IDBF"])
    dma_cast(MTRI[:], mtri_h, w=["MTRI"])
    dma_cast(BQ[:], bq_h, w=["BQb"])
    dma_cast(BK[:], bk_h, w=["BKb"])
    dma("sync", BVT[:], bvT_h, w=["BVT"])
    dma("sync", BOT[:], boT_h, w=["BOT"])
    for l in (0, 1):
        for k2 in ("b1a", "b1g", "b2T", "lngT", "lnbT"):
            dma("sync", SM[l][k2][:], ffn_h[l][k2], w=[f"SM{l}{k2}"])
    op("vector", lambda h: h.memset(ONESB[:], 1.0), w=["ONESB"])
    op("vector", lambda h: h.memset(ONECB[:], 1.0), w=["ONECB"])
    op("vector", lambda h: h.memset(ONES64[64:65, :], 1.0), w=["ONES64"])
    op("vector", lambda h: h.memset(CEPS[:], EPS), w=["CEPS"])
    op("vector", lambda h: h.memset(ONESF[:], 1.0), w=["ONESF"])
    op("vector", lambda h: h.memset(EIGHTF[:], 8.0), w=["EIGHTF"])

    def wload(dst, src, resname, cols, nchunk=8):
        s3 = src.rearrange("(m p) c -> p m c", p=128)
        dma_cast(dst[:].rearrange("p (m c) -> p m c", c=cols), s3,
                 w=[resname], tag=f"load {resname}")

    wload(WQ, wq_h, "WQ", D)
    wload(WK, wk_h, "WK", D)
    wload(WV, wv_h, "WV", D)
    wload(WOA, woa_h, "WOA", D)

    def ln_channel(T, read_tile, rsrc, rb, r2, mub, rsb, apply_cb,
                   c1=2048, c2=2560, tagp="ln"):
        rrb = tagp + "_rb"
        rr2 = tagp + "_r2"
        for mc in range(8):
            src = read_tile(mc)
            op("vector", lambda h, s=src: h.tensor_copy(rb[:, 0:T], s),
               r=rsrc, w=[rrb], tag=tagp + " rbcast")
            op("vector", lambda h: h.tensor_tensor(r2[:, 0:T], rb[:, 0:T],
                                                   rb[:, 0:T], ALU.mult),
               r=[rrb], w=[rr2], tag=tagp + " r2")
            op("tensor",
               lambda h, st=(mc == 0), sp=(mc == 7): h.matmul(
                   PS[0:1, c1:c1 + T], ONECB[:], rb[:, 0:T],
                   start=st, stop=sp),
               r=[rrb, "ONECB"], w=[tagp + "_s1"] + psb(c1, c1 + T),
               tag=tagp + " s1")
            op("tensor",
               lambda h, st=(mc == 0), sp=(mc == 7): h.matmul(
                   PS[0:1, c2:c2 + T], ONECB[:], r2[:, 0:T],
                   start=st, stop=sp),
               r=[rr2, "ONECB"], w=[tagp + "_s2"] + psb(c2, c2 + T),
               tag=tagp + " s2")
        op("vector", lambda h: h.tensor_scalar(
            ROW[0:1, 0:T], PS[0:1, c1:c1 + T], 1.0 / D, None, ALU.mult),
           r=[tagp + "_s1"] + psb(c1, c1 + T), w=[tagp + "_mu"], tag=tagp + " mu")
        op("vector", lambda h: h.tensor_scalar(
            ROW[0:1, 512:512 + T], PS[0:1, c2:c2 + T], 1.0 / D, None, ALU.mult),
           r=[tagp + "_s2"] + psb(c2, c2 + T), w=[tagp + "_m2"], tag=tagp + " m2")
        op("vector", lambda h: h.tensor_tensor(
            ROW[0:1, 1024:1024 + T], ROW[0:1, 0:T], ROW[0:1, 0:T], ALU.mult),
           r=[tagp + "_mu"], w=[tagp + "_v"], tag=tagp + " mu2")
        op("vector", lambda h: h.tensor_tensor(
            ROW[0:1, 1024:1024 + T], ROW[0:1, 512:512 + T],
            ROW[0:1, 1024:1024 + T], ALU.subtract),
           r=[tagp + "_m2", tagp + "_v"], w=[tagp + "_v"], tag=tagp + " var")
        op("scalar", lambda h: h.activation(
            ROW[0:1, 1024:1024 + T], ROW[0:1, 1024:1024 + T], AF.Ln, bias=EPS),
           r=[tagp + "_v", "CEPS"], w=[tagp + "_v"], tag=tagp + " ln")
        op("scalar", lambda h: h.activation(
            ROW[0:1, 1024:1024 + T], ROW[0:1, 1024:1024 + T], AF.Exp, scale=-0.5),
           r=[tagp + "_v"], w=[tagp + "_v"], tag=tagp + " rstd")
        op("tensor", lambda h: h.matmul(
            PS[:, c1:c1 + T], ONESF[:, 0:128], ROW[0:1, 0:T],
            start=True, stop=True),
           r=[tagp + "_mu"], w=[tagp + "_bc1"] + psb(c1, c1 + T), tag=tagp + " bc1")
        op("vector", lambda h: h.tensor_copy(mub[:, 0:T], PS[:, c1:c1 + T]),
           r=[tagp + "_bc1"] + psb(c1, c1 + T), w=[tagp + "_mub"], tag=tagp)
        op("tensor", lambda h: h.matmul(
            PS[:, c2:c2 + T], ONESF[:, 0:128], ROW[0:1, 1024:1024 + T],
            start=True, stop=True),
           r=[tagp + "_v"], w=[tagp + "_bc2"] + psb(c2, c2 + T), tag=tagp + " bc2")
        op("vector", lambda h: h.tensor_copy(rsb[:, 0:T], PS[:, c2:c2 + T]),
           r=[tagp + "_bc2"] + psb(c2, c2 + T), w=[tagp + "_rsb"], tag=tagp)
        for mc in range(8):
            apply_cb(mc, tagp + "_mub", tagp + "_rsb")

    def s1_load(cp):
        xc = XC[cp % 2]
        rXC = "XCs"
        xpt = XPT[cp % 2]
        rXP = "XPT0"
        src = xT_h.rearrange("(m p) t -> p m t", p=128)[:, :, 512 * cp:512 * cp + 512]
        dma("sync", xc[:].rearrange("p (m t) -> p m t", t=512), src,
            w=[rXC], tag=f"S1 load {cp}")
        for mc in range(8):
            for half in range(2):
                cb = mc * 512 + half * 256
                sap = xc[:, cb:cb + 256].rearrange("p (j t) -> p t j", t=4)
                dap = xpt[:, cb:cb + 256].rearrange("p (t j) -> p t j", j=64)
                op("gpsimd", lambda h, d=dap, s=sap: h.tensor_copy(d, s),
                   r=[rXC], w=[rXP], tag="S1 perm")

    def s1_qkv_groups(cp):
        xpt = XPT[cp % 2]
        rXP = "XPT0"
        for (W, res, BR, bres, dstl, dres) in (
            (WQ, "WQ", BQ, "BQb", QV, "QV"), (WK, "WK", BK, "BKb", KV, "KV"),
        ):
            for tau in range(4):
                for nh in range(2):
                    def emit(W=W, res=res, BR=BR, bres=bres, dstl=dstl,
                             dres=dres, tau=tau, nh=nh, xpt=xpt, rXP=rXP, cp=cp):
                        qc0 = 0 if (tau * 2 + nh) % 2 == 0 else 2048
                        pq = PS[:, qc0:qc0 + 512]
                        for mc in range(8):
                            l0 = xpt[:, mc * 512 + tau * 64:
                                     mc * 512 + tau * 64 + 64]
                            l1 = xpt[:, mc * 512 + 256 + tau * 64:
                                     mc * 512 + 256 + tau * 64 + 64]
                            rhs = W[:, mc * 1024 + nh * 512:
                                    mc * 1024 + nh * 512 + 512]
                            op("tensor",
                               lambda h, o_=pq, l0=l0, l1=l1, r_=rhs,
                                      st=(mc == 0):
                               (h.matmul(o_[0:64, :], l0, r_, start=st,
                                         stop=False, tile_position=(0, 0)),
                                h.matmul(o_[64:128, :], l1, r_, start=st,
                                         stop=False, tile_position=(0, 64)))[-1],
                               r=[rXP, res], w=psb(qc0, qc0 + 512),
                               tag=f"S1 {res} mm")
                        br = BR[:, nh * 512:nh * 512 + 512]
                        op("tensor",
                           lambda h, o_=pq, b=br:
                           (h.matmul(o_[0:64, :], ONESB[0:1, 0:64], b,
                                     start=False, stop=True,
                                     tile_position=(0, 0)),
                            h.matmul(o_[64:128, :], ONESB[0:1, 0:64], b,
                                     start=False, stop=True,
                                     tile_position=(0, 64)))[-1],
                           r=["ONESB", bres], w=psb(qc0, qc0 + 512),
                           tag="S1 bias")
                        op("vector",
                           lambda h, d=dstl[cp % 2], t_=tau, nh=nh, qc0=qc0:
                           h.tensor_copy(
                               d[:, t_ * 1024 + nh * 512:
                                 t_ * 1024 + nh * 512 + 512],
                               PS[:, qc0:qc0 + 512]),
                           r=psb(qc0, qc0 + 512), w=[dres + str(cp % 2)],
                           tag="S1 qkevac")
                    yield emit

    def s1_vt_groups(cp):
        xpt = XPT[cp % 2]
        rXP = "XPT0"
        vvt = VVT[cp % 2]
        rVV = f"VVT{cp % 2}"
        def emit_ones(vvt=vvt, rVV=rVV):
            op("vector",
               lambda h: h.memset(
                   vvt[:].rearrange("p (b r) -> p b r", r=65)[:, :, 64:65], 1.0),
               w=[rVV], tag="S1 vvt ones")
        yield emit_ones
        for g in range(2):
            for kap in range(8):
                def emit(g=g, kap=kap, xpt=xpt, rXP=rXP, vvt=vvt, rVV=rVV):
                    vc0 = 1024 if (g * 8 + kap) % 2 == 0 else 2560
                    pv = PS[:, vc0:vc0 + 256]
                    for mc in range(8):
                        lhs = WV[:, mc * 1024 + kap * 128:
                                 mc * 1024 + kap * 128 + 128]
                        rhs = xpt[:, mc * 512 + g * 256:mc * 512 + g * 256 + 256]
                        op("tensor",
                           lambda h, l_=lhs, r_=rhs, st=(mc == 0), sp=(mc == 7):
                           h.matmul(pv, l_, r_, start=st, stop=sp),
                           r=[rXP, "WV"], w=psb(vc0, vc0 + 256), tag="S1 vt mm")
                    dstap = vvt[:, g * 2080:g * 2080 + 2080].rearrange(
                        "p (t k r) -> p t k r", k=8, r=65)[:, :, kap, 0:64]
                    sap = pv.rearrange("p (t d2) -> p t d2", d2=64)
                    op("vector",
                       lambda h, d=dstap, s=sap, kap=kap: h.tensor_scalar(
                           d, s, BVT[:, kap:kap + 1], None, ALU.add),
                       r=psb(vc0, vc0 + 256) + ["BVT"], w=[rVV],
                       tag="S1 vt evac")
                yield emit

    def s1_attention(cp):
        qv = QV[cp % 2]
        kv = KV[cp % 2]
        vvt = VVT[cp % 2]
        rQ = f"QV{cp % 2}"
        rK = f"KV{cp % 2}"
        rVV = f"VVT{cp % 2}"

        def emit_S(kc, qb):
            c0 = 2048 + (kc % 2) * 1024
            rS = psb(c0, c0 + 1024)
            for g in range(2):
                gb = 64 * g
                lhs = kv[gb:gb + 64, kc * 128:kc * 128 + 128]
                rhs = qv[gb:gb + 64, qb * 512:qb * 512 + 512]
                o_ = PS[:, c0 + g * 512:c0 + g * 512 + 512]
                op("tensor",
                   lambda h, o2=o_, l_=lhs, r_=rhs, gb=gb:
                   h.matmul(o2, l_, r_, start=True, stop=True,
                            tile_position=(gb, 0)),
                   r=[rK, rQ], w=psb(c0 + g * 512, c0 + g * 512 + 512),
                   tag="S1 S")
            op("tensor",
               lambda h: h.matmul(PS[:, 0:512], WQ[:, 0:128], WQ[:, 0:512],
                                  start=True, stop=True),
               r=["WQ"], w=psb(0, 512), tag="S1 warm")
            op("scalar",
               lambda h, d=P2[kc % 2], s=PS[:, c0:c0 + 1024]: h.activation(
                   d[:], s, AF.Exp, scale=1.0 / 1024.0),
               r=rS, w=[f"P2{kc % 2}"], tag="S1 exp")

        def emit_PV(kc):
            p2 = P2[kc % 2]
            rP = f"P2{kc % 2}"
            for g in range(2):
                ab = 512 + 1024 * g
                for qs in range(4):
                    lhs = p2[:, g * 512 + qs * 128:g * 512 + qs * 128 + 128]
                    rhs = vvt[:, g * 2080 + kc * 65:g * 2080 + kc * 65 + 65]
                    op("tensor",
                       lambda h, l_=lhs, r_=rhs, ab=ab, qs=qs,
                              st=(kc == 0 and qs == 0), sp=(kc == 31 and qs == 3):
                       h.matmul(PS[:, ab + qs * 65:ab + qs * 65 + 65], l_, r_,
                                start=st, stop=sp),
                       r=[rP, rVV], w=psb(ab, ab + 260), tag="S1 PV")

        def emit_tail(qb):
            for g in range(2):
                ab = 512 + 1024 * g
                gb = 64 * g
                rA = psb(ab, ab + 260)
                rc = PS[:, ab:ab + 260].rearrange(
                    "p (q e) -> p q e", e=65)[:, :, 64:65]
                op("vector", lambda h, rc=rc, g=g: h.reciprocal(
                    NMX[:, 8 + 4 * g:12 + 4 * g].rearrange(
                        "p (q e) -> p q e", e=1), rc),
                   r=rA, w=[f"NMXt{g}"], tag="S1 arcp")
                for qs in range(4):
                    op("vector", lambda h, qs=qs, ab=ab, g=g: h.tensor_scalar(
                        ATOK[:, qs * 64:qs * 64 + 64],
                        PS[:, ab + qs * 65:ab + qs * 65 + 64],
                        NMX[:, 8 + 4 * g + qs:9 + 4 * g + qs], None, ALU.mult),
                       r=rA + [f"NMXt{g}"], w=["ATOK"], tag="S1 anrm")
                for qs in range(4):
                    pt = PSH[0:64, 2 * ab + qs * 128:2 * ab + qs * 128 + 128]
                    op("tensor", lambda h, o2=pt, qs=qs:
                       h.transpose(o2, ATOK[:, qs * 64:qs * 64 + 64], IDBF[:]),
                       r=["ATOK", "IDBF"], w=rA, tag="S1 atr")
                src = PSH[0:64, 2 * ab:2 * ab + 512]
                op("vector",
                   lambda h, s2=src, gb=gb, qb=qb: h.tensor_copy(
                       ANB[cp % 2][gb:gb + 64, qb * 512:qb * 512 + 512], s2),
                   r=rA, w=[f"ANB{cp % 2}"], tag="S1 aev")

        state = {"pend": None}
        blocks = []
        for qb in range(8):
            def blk(qb=qb):
                for kc in range(32):
                    emit_S(kc, qb)
                    if kc == 1 and state["pend"] is not None:
                        emit_tail(state["pend"])
                        state["pend"] = None
                    if kc > 0:
                        emit_PV(kc - 1)
                emit_PV(31)
                state["pend"] = qb
            blocks.append(blk)
        blocks.append(lambda: emit_tail(state["pend"]))
        return blocks

    def s1_unscr(cp):
        for g in range(2):
            for p_ in range(16):
                kap = p_ // 2
                rb_ = 64 * (p_ % 2)
                sap = ANB[cp % 2][g * 64:g * 64 + 64, :].rearrange(
                    "p (t s) -> p t s", s=16)[:, :, p_:p_ + 1]
                op("vector",
                   lambda h, s=sap, kap=kap, rb2=rb_, g=g: h.tensor_copy(
                       UNS[rb2:rb2 + 64,
                           kap * 512 + g * 256:kap * 512 + g * 256 + 256]
                       .rearrange("p (t o) -> p t o", o=1), s),
                   r=[f"ANB{cp % 2}"], w=["UNS"], tag="S1 unscr")

    def s1_wo_groups(cp):
        for ct in range(8):
            def emit(ct=ct, cp=cp):
                xr = XR[0]
                rXR = "XRs"
                dma("sync", xr[:],
                    xT_h[ct * 128:ct * 128 + 128, 512 * cp:512 * cp + 512],
                    w=[rXR], tag="S1 xr")
                wc0 = 3072 if ct % 2 == 0 else 3584
                pw = PS[:, wc0:wc0 + 512]
                for nch in range(8):
                    lhs = WOA[:, nch * 1024 + ct * 128:
                              nch * 1024 + ct * 128 + 128]
                    rhs = UNS[:, nch * 512:nch * 512 + 512]
                    op("tensor",
                       lambda h, l_=lhs, r_=rhs, st=(nch == 0), sp=(nch == 7):
                       h.matmul(pw, l_, r_, start=st, stop=sp),
                       r=["UNS", "WOA"], w=psb(wc0, wc0 + 512), tag="S1 wo")
                op("vector",
                   lambda h, ct=ct, xr=xr, pw=pw: h.scalar_tensor_tensor(
                       RT[:, ct * 512:ct * 512 + 512], pw,
                       BOT[:, ct:ct + 1], xr[:],
                       ALU.add, ALU.add),
                   r=psb(wc0, wc0 + 512) + ["BOT", rXR], w=["RT"], tag="S1 res")
            yield emit

    def s1_ln(cp):
        def s1_apply(mc, rmub, rrsb, cp=cp):
            xt = X1T[0]
            rX1 = "X1Ts"
            op("vector",
               lambda h, mc=mc, xt=xt: h.tensor_tensor(
                   xt[:], RT[:, mc * 512:mc * 512 + 512], MUB[:], ALU.subtract),
               r=["RT", rmub], w=[rX1], tag="S1 lnap1")
            op("vector",
               lambda h, xt=xt: h.tensor_tensor(xt[:], xt[:], RSB[:], ALU.mult),
               r=[rX1, rrsb], w=[rX1], tag="S1 lnap2")
            dma("scalar", xS["xA"][mc * 128:mc * 128 + 128,
                                   512 * cp:512 * cp + 512], xt[:],
                r=[rX1], w=[f"hbm_xA_{mc}_{cp}"], tag="S1 out")

        ln_channel(512, lambda mc: RT[:, mc * 512:mc * 512 + 512], ["RT"],
                   RBT, R2T, MUB, RSB, s1_apply, tagp=f"s1ln{cp}")

    s1_load(0)
    for p in range(8):
        qkv = list(s1_qkv_groups(p))
        vt = list(s1_vt_groups(p))
        woq = list(s1_wo_groups(p - 1)) if p > 0 else []
        if p > 0:
            s1_unscr(p - 1)
        front = []
        for i in range(max(len(qkv), len(woq))):
            if i < len(qkv):
                front.append(qkv[i])
            if i < len(woq):
                front.append(woq[i])
        front.extend(vt)
        for f in front:
            f()
        if p < 7:
            s1_load(p + 1)
        if p > 0:
            s1_ln(p - 1)
        for b in s1_attention(p):
            b()
    s1_unscr(7)
    for f in s1_wo_groups(7):
        f()
    s1_ln(7)


    def build_tables(l):
        dma("sync", ROW[0:1, 0:2048], ffn_h[l]["qsrep"], w=["ROWq"], tag="qsrep")
        for j in range(4):
            op("tensor", lambda h, j=j: h.matmul(
                PS[:, 3072 + 0:3072 + 512], EIGHTF[0:1, 0:128],
                ROW[0:1, j * 512:j * 512 + 512], start=True, stop=True),
               r=["ROWq"], w=psb(3072, 3584), tag="qs bc")
            op("vector", lambda h, j=j: h.tensor_copy(
                QSB[:, j * 512:j * 512 + 512], PS[:, 3072:3584]),
               r=psb(3072, 3584), w=["QSB"], tag="qs ev")
        dma("sync", TRAW[:], tab_h["cs"], w=["TRAW"], tag="cs")
        op("vector", lambda h: h.tensor_tensor(T1Q[:], TRAW[:], QSB[:], ALU.mult),
           r=["TRAW", "QSB"], w=["T1Q"], tag="T1Q")
        qsb_hi = QSB[:].rearrange("p (t d) -> p t d", d=64)[:, :, 32:64]
        qsb_lo = QSB[:].rearrange("p (t d) -> p t d", d=64)[:, :, 0:32]
        dma("sync", TRAW[:], tab_h["ss"], r=["T1Q"], w=["TRAW"], tag="ss")

        def t2build(T2, rT2):
            t2_lo = T2[:].rearrange("p (t d) -> p t d", d=64)[:, :, 0:32]
            t2_hi = T2[:].rearrange("p (t d) -> p t d", d=64)[:, :, 32:64]
            ss_lo = TRAW[:].rearrange("p (t d) -> p t d", d=64)[:, :, 0:32]
            ss_hi = TRAW[:].rearrange("p (t d) -> p t d", d=64)[:, :, 32:64]
            op("vector", lambda h: h.scalar_tensor_tensor(
                t2_lo, ss_lo, -1.0, qsb_hi, ALU.mult, ALU.mult),
               r=["TRAW", "QSB"], w=[rT2], tag="T2lo")
            op("vector", lambda h: h.tensor_tensor(t2_hi, ss_hi, qsb_lo,
                                                   ALU.mult),
               r=["TRAW", "QSB"], w=[rT2], tag="T2hi")

        t2build(T2Q, "T2Q")
        dma("sync", ROW[0:1, 0:2048], ffn_h[l]["ksrep"], r=["ROWq"], w=["ROWq"],
            tag="ksrep")
        for j in range(4):
            op("tensor", lambda h, j=j: h.matmul(
                PS[:, 3072:3584], ONESF[0:1, 0:128],
                ROW[0:1, j * 512:j * 512 + 512], start=True, stop=True),
               r=["ROWq"], w=psb(3072, 3584), tag="ks bc")
            op("vector", lambda h, j=j: h.tensor_copy(
                QSB[:, j * 512:j * 512 + 512], PS[:, 3072:3584]),
               r=psb(3072, 3584), w=["QSB"], tag="ks ev")
        dma("sync", TRAW[:], tab_h["ci"], r=["T2Q"], w=["TRAW"], tag="ci")
        op("vector", lambda h: h.tensor_tensor(T1K[:], TRAW[:], QSB[:], ALU.mult),
           r=["TRAW", "QSB"], w=["T1K"], tag="T1K")
        dma("sync", TRAW[:], tab_h["si"], r=["T1K"], w=["TRAW"], tag="si")
        t2build(T2K, "T2K")
        if debug and l == 0:
            dbg("T1Q", T1Q[:], ["T1Q"])
            dbg("T2Q", T2Q[:], ["T2Q"])
            dbg("T1K", T1K[:], ["T1K"])
            dbg("T2K", T2K[:], ["T2K"])

    def stage_f1(l, x_in, x_out):
        wload(WQKV, ffn_h[l]["wqkv"], "WQKV", 1536)
        wof3 = ffn_h[l]["wof"].rearrange("(m p) c -> p m c", p=128)
        dma_cast(WOF[:].rearrange("p (m c) -> p m c", c=1024), wof3, w=["WOF"],
                 tag="load WOF")
        build_tables(l)

        def f1_load(w_):
            if w_ == 0:
                op("vector", lambda h: h.memset(
                    XW[:].rearrange("p (m t) -> p m t", t=513)[:, :, 0:1], 0.0),
                   w=["XW"], tag="F1 xw0")
                src = x_in.rearrange("(m p) t -> p m t", p=128)[:, :, 0:512]
                dma("sync",
                    XW[:].rearrange("p (m t) -> p m t", t=513)[:, :, 1:513],
                    src, w=["XW"], tag="F1 loadx")
            else:
                src = x_in.rearrange("(m p) t -> p m t", p=128)[
                    :, :, 512 * w_ - 1:512 * w_ + 512]
                dma("sync", XW[:].rearrange("p (m t) -> p m t", t=513), src,
                    w=["XW"], tag="F1 loadx")

        def f1_front_pieces(w_):
            slot = w_ % 3
            rV = f"VRING{slot}"
            rK2 = f"KT2_{slot}"
            rQ2 = f"QT2_{w_ % 2}"
            qt2 = QT2[w_ % 2]
            kt2 = KT2[slot]
            inb = INB[w_ % 2]
            rI = f"INB{w_ % 2}"
            pieces = []

            def ht_piece():
                for mc in range(8):
                    off = 1 if mc < 4 else 0
                    sap = XW[:, mc * 513 + off:mc * 513 + off + 512]
                    op("gpsimd", lambda h, s=sap, mc=mc: h.tensor_copy(
                        HT[:, mc * 512:mc * 512 + 512], s),
                       r=["XW"], w=["HT"], tag="F1 ht")
            pieces.append(ht_piece)

            def qkv_piece(tt, nh):
                qc = (nh if nh < 2 else 3) * 512
                pq = PS[:, qc:qc + 512]
                for mc in range(8):
                    lhs = HT[:, mc * 512 + tt * 128:mc * 512 + tt * 128 + 128]
                    rhs = WQKV[:, mc * 1536 + nh * 512:mc * 1536 + nh * 512 + 512]
                    op("tensor",
                       lambda h, o_=pq, l_=lhs, r_=rhs, st=(mc == 0),
                              sp=(mc == 7):
                       h.matmul(o_, l_, r_, start=st, stop=sp),
                       r=["HT", "WQKV"], w=psb(qc, qc + 512),
                       tag="F1 qkv")
                if nh == 0:
                    op("scalar", lambda h, tt=tt: h.copy(
                        QF[:, tt * 512:tt * 512 + 512], PS[:, 0:512]),
                       r=psb(0, 512), w=["QF"], tag="F1 qev")
                elif nh == 1:
                    op("scalar", lambda h, tt=tt: h.copy(
                        KF[:, tt * 512:tt * 512 + 512], PS[:, 512:1024]),
                       r=psb(512, 1024), w=["KF"], tag="F1 kev")
                else:
                    vd = VRING[slot][:, tt * 520:tt * 520 + 520].rearrange(
                        "p (hh r) -> p hh r", r=65)[:, :, 0:64]
                    vs = PS[:, 1536:2048].rearrange("p (hh d) -> p hh d", d=64)
                    op("scalar", lambda h, d=vd, s2=vs: h.copy(d, s2),
                       r=psb(1536, 2048), w=[rV], tag="F1 vev")
            for tt in range(4):
                for nh in range(3):
                    pieces.append(lambda tt=tt, nh=nh: qkv_piece(tt, nh))

            def vones():
                op("vector", lambda h, s2=slot: h.memset(
                    VRING[s2][:].rearrange("p (b r) -> p b r", r=65)[:, :, 64:65],
                    1.0),
                   w=[rV], tag="F1 vones")
            pieces.append(vones)

            if debug and l == 0 and w_ == 0:
                def dbgs():
                    dbg("HT", HT[:], ["HT"])
                    dbg("QF", QF[:], ["QF"])
                    dbg("KF", KF[:], ["KF"])
                    dbg("VR0", VRING[0][:], ["VRING0"])
                pieces.append(dbgs)

            def norm_piece(which):
                (xf, rxf, col0) = ((QF, "QF", 0), (KF, "KF", 32))[which]
                op("vector", lambda h, x2=xf: h.tensor_tensor(
                    QR[:], x2[:], x2[:], ALU.mult),
                   r=[rxf], w=["QR"], tag="F1 sq")
                op("vector", lambda h, c0=col0: h.reduce_sum(
                    STATC[:, c0:c0 + 32].rearrange("p (g o) -> p g o", o=1),
                    QR[:].rearrange("p (g d) -> p g d", d=64), axis=AX.X),
                   r=["QR"], w=["STATC"], tag="F1 ss")
                op("scalar", lambda h, c0=col0: h.activation(
                    STATC[:, c0:c0 + 32], STATC[:, c0:c0 + 32], AF.Ln),
                   r=["STATC"], w=["STATC"], tag="F1 lnn")
                op("scalar", lambda h, c0=col0: h.activation(
                    STATC[:, c0:c0 + 32], STATC[:, c0:c0 + 32], AF.Exp,
                    scale=0.5),
                   r=["STATC"], w=["STATC"], tag="F1 nrm")
                op("vector", lambda h, c0=col0: h.tensor_scalar(
                    STATC[:, c0:c0 + 32], STATC[:, c0:c0 + 32], 1e-12, None,
                    ALU.max),
                   r=["STATC"], w=["STATC"], tag="F1 clip")
                op("vector", lambda h, c0=col0: h.reciprocal(
                    inb[:, c0:c0 + 32], STATC[:, c0:c0 + 32]),
                   r=["STATC"], w=[rI], tag="F1 inb")
            pieces.append(lambda: norm_piece(0))
            pieces.append(lambda: norm_piece(1))

            def rot_piece(which, part):
                (xf, rxf, T1, rT1, T2, rT2, ur, rur, vr, rvr, hb, rhb, eng) = (
                    (QF, "QF", T1Q, "T1Q", T2Q, "T2Q", QR, "QR", VRA, "VRA",
                     QHB, "QHB", "gpsimd"),
                    (KF, "KF", T1K, "T1K", T2K, "T2K", KR, "KR", VRB, "VRB",
                     KHB, "KHB", "gpsimd"),
                )[which]
                tts = 4 * w_

                def hslice(buf, hh, d0, d1):
                    return buf[:].rearrange("p (t g d) -> p t g d", g=8,
                                            d=64)[:, :, hh, d0:d1]

                t1ap = T1[:, tts * 64:tts * 64 + 256].rearrange(
                    "p (t d) -> p t d", d=64)
                t2lo = T2[:, tts * 64:tts * 64 + 256].rearrange(
                    "p (t d) -> p t d", d=64)[:, :, 0:32]
                t2hi = T2[:, tts * 64:tts * 64 + 256].rearrange(
                    "p (t d) -> p t d", d=64)[:, :, 32:64]
                if part < 4:
                    for hh in range(2 * part, 2 * part + 2):
                        op(eng, lambda h, o_=hslice(ur, hh, 0, 64),
                               a=hslice(xf, hh, 0, 64), b=t1ap:
                           h.tensor_tensor(o_, a, b, ALU.mult),
                           r=[rxf, rT1], w=[rur], tag="F1 rotu")
                        op(eng, lambda h, o_=hslice(vr, hh, 0, 32),
                               a=hslice(xf, hh, 32, 64), b=t2lo:
                           h.tensor_tensor(o_, a, b, ALU.mult),
                           r=[rxf, rT2], w=[rvr], tag="F1 rotvA")
                        op(eng, lambda h, o_=hslice(vr, hh, 32, 64),
                               a=hslice(xf, hh, 0, 32), b=t2hi:
                           h.tensor_tensor(o_, a, b, ALU.mult),
                           r=[rxf, rT2], w=[rvr], tag="F1 rotvB")
                else:
                    op(eng, lambda h, o_=hb, a=ur, b=vr:
                       h.tensor_tensor(o_[:], a[:], b[:], ALU.add),
                       r=[rur, rvr], w=[rhb], tag="F1 rotadd")
                    if which == 1:
                        for g32 in range(32):
                            tt, hh = divmod(g32, 8)
                            sl = KHB[:, tt * 512 + hh * 64:
                                     tt * 512 + hh * 64 + 64]
                            op("vector", lambda h, s2=sl, g32=g32:
                               h.tensor_scalar(
                                   s2, s2, inb[:, 32 + g32:32 + g32 + 1], None,
                                   ALU.mult),
                               r=["KHB", rI], w=["KHB"], tag="F1 kinb")
            for part in range(5):
                pieces.append(lambda part=part: rot_piece(0, part))
            for part in range(5):
                pieces.append(lambda part=part: rot_piece(1, part))

            def tr_piece(which, hp):
                (hb, rhb, dstT, rdst) = ((QHB, "QHB", qt2, rQ2),
                                         (KHB, "KHB", kt2, rK2))[which]
                for tt in range(4):
                    pt = PSH[:, 3072 + tt * 128:3072 + tt * 128 + 128]
                    op("tensor", lambda h, o_=pt, s2=hb, tt=tt, hp=hp:
                       h.transpose(o_, s2[:, tt * 512 + hp * 128:
                                        tt * 512 + hp * 128 + 128], IDBF[:]),
                       r=[rhb, "IDBF"], w=psb(1536, 2048), tag="F1 tr")
                src = PSH[:, 3072:3584]
                op("vector", lambda h, d=dstT, hp=hp, s2=src: h.tensor_copy(
                    d[:, hp * 512:hp * 512 + 512], s2),
                   r=psb(1536, 2048), w=[rdst], tag="F1 trev")
            for which in range(2):
                for hp in range(4):
                    pieces.append(lambda which=which, hp=hp:
                                  tr_piece(which, hp))
            return pieces

        def f1_attn_blocks(w_):
            qt2 = QT2[w_ % 2]
            rQ2 = f"QT2_{w_ % 2}"
            inb = INB[w_ % 2]
            rI = f"INB{w_ % 2}"

            def f1_attn_S(hp, qt, g):
                gb = 64 * g
                kb_lo = max(0, 4 * w_ + qt - 4)
                kb_hi = 4 * w_ + qt
                nb = kb_hi - kb_lo + 1
                sc0 = 2048 + 768 * g
                spsum = PS[:, sc0:sc0 + 1024]
                rS = psb(sc0, sc0 + nb * 128)
                lhs = qt2[gb:gb + 64, hp * 512 + qt * 128:
                          hp * 512 + qt * 128 + 128]
                for jj, kb in enumerate(range(kb_lo, kb_hi + 1)):
                    ksl = (kb // 4) % 3
                    rhs = KT2[ksl][gb:gb + 64,
                                   hp * 512 + (kb % 4) * 128:
                                   hp * 512 + (kb % 4) * 128 + 128]
                    o_ = spsum[:, jj * 128:jj * 128 + 128]
                    op("tensor", lambda h, o2=o_, l_=lhs, r_=rhs,
                              sp=(kb != kb_hi):
                       h.matmul(o2, l_, r_, start=True, stop=sp),
                       r=[rQ2, f"KT2_{ksl}"], w=rS, tag="F1 S")
                o_ = spsum[:, (nb - 1) * 128:(nb - 1) * 128 + 128]
                op("tensor", lambda h, o2=o_: h.matmul(
                    o2, IDBF[:], MTRI[:], start=False, stop=True),
                   r=["IDBF", "MTRI"], w=rS, tag="F1 mask")
                hh = 2 * hp + g
                g32 = qt * 8 + hh
                pbuf = PB[g + 2 * (qt % 2)]
                rPB = f"PB{g + 2 * (qt % 2)}"
                op("vector", lambda h, s2=spsum, nb=nb, g=g: h.reduce_max(
                    NMX[:, g:g + 1], s2[:, 0:nb * 128], axis=AX.X,
                    negate=True),
                   r=rS, w=[f"NMXr{g}"], tag="F1 max")
                op("vector", lambda h, g=g, g32=g32: h.tensor_tensor(
                    NMX[:, 2 + g:3 + g], NMX[:, g:g + 1],
                    inb[:, g32:g32 + 1], ALU.mult),
                   r=[f"NMXr{g}", rI], w=[f"NMXs{g}"], tag="F1 mxs")
                op("scalar", lambda h, d=pbuf, s2=spsum, nb=nb, g=g,
                             g32=g32: h.activation(
                    d[:, 0:nb * 128], s2[:, 0:nb * 128], AF.Exp,
                    bias=NMX[:, 2 + g:3 + g],
                    scale=inb[:, g32:g32 + 1]),
                   r=rS + [f"NMXs{g}", rI], w=[rPB], tag="F1 exp")
                if w_ > 0:
                    op("gpsimd", lambda h, d=pbuf: h.affine_select(
                        out=d[:, 0:128], in_=d[:, 0:128],
                        compare_op=ALU.is_ge, fill=0.0, base=0,
                        pattern=[[1, 128]], channel_multiplier=-1),
                       r=[rPB], w=[rPB], tag="F1 asel")

            def f1_attn_tail(hp, qt, g):
                hh = 2 * hp + g
                kb_lo = max(0, 4 * w_ + qt - 4)
                kb_hi = 4 * w_ + qt
                nb = kb_hi - kb_lo + 1
                pbuf = PB[g + 2 * (qt % 2)]
                rPB = f"PB{g + 2 * (qt % 2)}"
                ptb = PTB[g]
                rPT = f"PTB{g}"
                sb0, sf0, sf1 = ((3072, 1536, 2048) if qt % 2 == 0
                                 else (2048, 1024, 1536))
                for jj in range(nb):
                    pt = PSH[:, sb0 + jj * 128:sb0 + jj * 128 + 128]
                    op("tensor", lambda h, o2=pt, d=pbuf, jj=jj:
                       h.transpose(o2, d[:, jj * 128:jj * 128 + 128],
                                   IDBF[:]),
                       r=[rPB, "IDBF"], w=psb(sf0, sf1), tag="F1 ptr")
                src = PSH[:, sb0:sb0 + nb * 128]
                op("vector", lambda h, d=ptb, s2=src, nb=nb: h.tensor_copy(
                    d[:, 0:nb * 128], s2),
                   r=psb(sf0, sf1), w=[rPT], tag="F1 ptev")
                oc0 = 3584 + 128 * g
                ops_ = PS[:, oc0:oc0 + 65]
                for jj, kb in enumerate(range(kb_lo, kb_hi + 1)):
                    ksl = (kb // 4) % 3
                    rhs = VRING[ksl][:, (kb % 4) * 520 + hh * 65:
                                     (kb % 4) * 520 + hh * 65 + 65]
                    op("tensor", lambda h, o2=ops_, d=ptb, jj=jj, r_=rhs,
                              st=(jj == 0), sp=(jj == nb - 1):
                       h.matmul(o2, d[:, jj * 128:jj * 128 + 128], r_,
                                start=st, stop=sp),
                       r=[rPT, f"VRING{ksl}"], w=psb(oc0, oc0 + 65),
                       tag="F1 PV")
                op("vector", lambda h, o2=ops_, g=g: h.reciprocal(
                    NMX[:, 4 + g:5 + g], o2[:, 64:65]),
                   r=psb(oc0, oc0 + 65), w=[f"NMXo{g}"], tag="F1 orcp")
                op("scalar", lambda h, o2=ops_, qt=qt, hh=hh, g=g:
                   h.activation(
                       OW[:, qt * 512 + hh * 64:qt * 512 + hh * 64 + 64],
                       o2[:, 0:64], AF.Copy,
                       scale=NMX[:, 4 + g:5 + g]),
                   r=psb(oc0, oc0 + 65) + [f"NMXo{g}"], w=["OW"],
                   tag="F1 onrm")

            blocks = []
            state = {"pend": None}
            for hp in range(4):
                for qt in range(4):
                    def blk(hp=hp, qt=qt):
                        f1_attn_S(hp, qt, 0)
                        f1_attn_S(hp, qt, 1)
                        if state["pend"] is not None:
                            f1_attn_tail(*state["pend"])
                            f1_attn_tail(state["pend"][0], state["pend"][1], 1)
                        state["pend"] = (hp, qt, 0)
                    blocks.append(blk)

            def last():
                f1_attn_tail(*state["pend"])
                f1_attn_tail(state["pend"][0], state["pend"][1], 1)
            blocks.append(last)
            return blocks

        def f1_back(w_):
            if debug and l == 0 and w_ == 0:
                dbg("OW", OW[:], ["OW"])
            for hp in range(4):
                for tt in range(4):
                    pt = PSH[:, 3072 + tt * 128:3072 + tt * 128 + 128]
                    op("tensor", lambda h, o_=pt, tt=tt, hp=hp:
                       h.transpose(o_, OW[:, tt * 512 + hp * 128:
                                         tt * 512 + hp * 128 + 128], IDBF[:]),
                       r=["OW", "IDBF"], w=psb(1536, 2048), tag="F1 otr")
                src = PSH[:, 3072:3584]
                op("vector", lambda h, hp=hp, s2=src: h.tensor_copy(
                    OT[:, hp * 512:hp * 512 + 512], s2),
                   r=psb(1536, 2048), w=["OT"], tag="F1 otev")
            for ct in range(8):
                xr = XR2[ct % 2]
                rXR = f"XR2_{ct % 2}"
                dma("sync", xr[:],
                    x_in[ct * 128:ct * 128 + 128, 512 * w_:512 * w_ + 512],
                    w=[rXR], tag="F1 xr")
                pw = PS[:, 1024:1536]
                for nch in range(4):
                    lhs = WOF[:, nch * 1024 + ct * 128:nch * 1024 + ct * 128 + 128]
                    rhs = OT[:, nch * 512:nch * 512 + 512]
                    op("tensor",
                       lambda h, l_=lhs, r_=rhs, st=(nch == 0), sp=(nch == 3):
                       h.matmul(pw, l_, r_, start=st, stop=sp),
                       r=["OT", "WOF"], w=psb(1024, 1536), tag="F1 wo")
                x2 = X2T[ct % 2]
                rX2 = f"X2T{ct % 2}"
                op("vector", lambda h, ct=ct, x2=x2, xr=xr: h.tensor_tensor(
                    x2[:], PS[:, 1024:1536], xr[:], ALU.add),
                   r=psb(1024, 1536) + [rXR], w=[rX2], tag="F1 res")
                dma("scalar", x_out[ct * 128:ct * 128 + 128,
                                    512 * w_:512 * w_ + 512], x2[:],
                    r=[rX2], w=[f"hbm_f1_{ct}_{w_}"], tag="F1 out")

        f1_load(0)
        for f in f1_front_pieces(0):
            f()
        for w_ in range(8):
            blocks = f1_attn_blocks(w_)
            pieces = []
            if w_ < 7:
                f1_load(w_ + 1)
                pieces = f1_front_pieces(w_ + 1)
            k = 0
            per = (len(pieces) + len(blocks) - 1) // len(blocks)
            for b in blocks:
                b()
                for _ in range(per):
                    if k < len(pieces):
                        pieces[k]()
                        k += 1
            while k < len(pieces):
                pieces[k]()
                k += 1
            f1_back(w_)

    def stage_f2(l, x_in, x_out):
        w13 = ffn_h[l]["w1"].rearrange("(m p) c -> p m c", p=128)
        dma_cast(W1S[:].rearrange("p (m c) -> p m c", c=5460), w13, w=["W1S"],
                 tag="load W1")
        w23 = ffn_h[l]["w2"]
        for jn in range(22):
            r0 = 128 * jn
            r1 = min(2730, r0 + 128)
            dma_cast(W2S[0:r1 - r0, jn * 1024:jn * 1024 + 1024], w23[r0:r1, :],
                     w=["W2S"], tag="load W2")

        def f2_front(c):
            xc2 = XC2[c % 2]
            rX = f"XC2_{c % 2}"
            h2b = H2B[c % 2]
            rH = f"H2B_{c % 2}"
            if c == 0:
                op("vector", lambda h, xc2=xc2: h.memset(
                    xc2[:].rearrange("p (m t) -> p m t", t=257)[:, :, 0:1], 0.0),
                   w=[rX], tag="F2 xc0")
                src = x_in.rearrange("(m p) t -> p m t", p=128)[:, :, 0:256]
                dma("sync",
                    xc2[:].rearrange("p (m t) -> p m t", t=257)[:, :, 1:257],
                    src, w=[rX], tag="F2 loadx")
            else:
                src = x_in.rearrange("(m p) t -> p m t", p=128)[
                    :, :, 256 * c - 1:256 * c + 256]
                dma("sync", xc2[:].rearrange("p (m t) -> p m t", t=257), src,
                    w=[rX], tag="F2 loadx")

            def h2read(mc):
                off = 1 if mc < 4 else 0
                return xc2[:, mc * 257 + off:mc * 257 + off + 256]

            for mc in range(8):
                op("gpsimd", lambda h, mc=mc, s=h2read(mc), h2b=h2b:
                   h.tensor_copy(h2b[:, mc * 256:mc * 256 + 256], s),
                   r=[rX], w=[rH], tag="F2 h2b")

            tagp = f"f2ln{l}_{c}"
            for mc in range(8):
                op("vector", lambda h, mc=mc, h2b=h2b: h.tensor_tensor(
                    R2C[:], h2b[:, mc * 256:mc * 256 + 256],
                    h2b[:, mc * 256:mc * 256 + 256], ALU.mult),
                   r=[rH], w=["R2C"], tag="F2 r2")
                op("tensor", lambda h, mc=mc, h2b=h2b, st=(mc == 0),
                          sp=(mc == 7):
                   h.matmul(PS[0:1, 3072:3328], ONECB[:],
                            h2b[:, mc * 256:mc * 256 + 256], start=st, stop=sp),
                   r=[rH, "ONECB"], w=[tagp + "_s1"] + psb(3072, 3328),
                   tag="F2 s1")
                op("tensor", lambda h, st=(mc == 0), sp=(mc == 7):
                   h.matmul(PS[0:1, 3584:3840], ONECB[:], R2C[:],
                            start=st, stop=sp),
                   r=["R2C", "ONECB"], w=[tagp + "_s2"] + psb(3584, 3840),
                   tag="F2 s2")
            op("vector", lambda h: h.tensor_scalar(
                ROW[0:1, 0:256], PS[0:1, 3072:3328], 1.0 / D, None, ALU.mult),
               r=[tagp + "_s1"] + psb(3072, 3328), w=[tagp + "_mu"], tag="F2 mu")
            op("vector", lambda h: h.tensor_scalar(
                ROW[0:1, 512:768], PS[0:1, 3584:3840], 1.0 / D, None, ALU.mult),
               r=[tagp + "_s2"] + psb(3584, 3840), w=[tagp + "_m2"], tag="F2 m2")
            op("vector", lambda h: h.tensor_tensor(
                ROW[0:1, 1024:1280], ROW[0:1, 0:256], ROW[0:1, 0:256], ALU.mult),
               r=[tagp + "_mu"], w=[tagp + "_v"], tag="F2 mu2")
            op("vector", lambda h: h.tensor_tensor(
                ROW[0:1, 1024:1280], ROW[0:1, 512:768], ROW[0:1, 1024:1280],
                ALU.subtract),
               r=[tagp + "_m2", tagp + "_v"], w=[tagp + "_v"], tag="F2 var")
            op("scalar", lambda h: h.activation(
                ROW[0:1, 1024:1280], ROW[0:1, 1024:1280], AF.Ln, bias=EPS),
               r=[tagp + "_v", "CEPS"], w=[tagp + "_v"], tag="F2 ln")
            op("scalar", lambda h: h.activation(
                ROW[0:1, 1024:1280], ROW[0:1, 1024:1280], AF.Exp, scale=-0.5),
               r=[tagp + "_v"], w=[tagp + "_v"], tag="F2 rstd")
            op("tensor", lambda h: h.matmul(
                PS[:, 3072:3328], ONESF[0:1, 0:128], ROW[0:1, 0:256],
                start=True, stop=True),
               r=[tagp + "_mu"], w=[tagp + "_b1"] + psb(3072, 3328),
               tag="F2 bc1")
            op("vector", lambda h: h.tensor_copy(MUB2[:], PS[:, 3072:3328]),
               r=[tagp + "_b1"] + psb(3072, 3328), w=[tagp + "_mub"], tag="F2")
            op("tensor", lambda h: h.matmul(
                PS[:, 3584:3840], ONESF[0:1, 0:128], ROW[0:1, 1024:1280],
                start=True, stop=True),
               r=[tagp + "_v"], w=[tagp + "_b2"] + psb(3584, 3840),
               tag="F2 bc2")
            op("vector", lambda h: h.tensor_copy(RSB2[:], PS[:, 3584:3840]),
               r=[tagp + "_b2"] + psb(3584, 3840), w=[tagp + "_rsb"], tag="F2")
            for mc in range(8):
                op("vector", lambda h, mc=mc, s=h2read(mc): h.tensor_tensor(
                    GTMP[:, 0:256], s, MUB2[:], ALU.subtract),
                   r=[rX, tagp + "_mub"], w=["GTMP0"], tag="F2 ap1")
                op("vector", lambda h: h.tensor_tensor(
                    GTMP[:, 0:256], GTMP[:, 0:256], RSB2[:], ALU.mult),
                   r=["GTMP0", tagp + "_rsb"], w=["GTMP0"], tag="F2 ap2")
                op("vector", lambda h, mc=mc, l=l, h2b=h2b: h.tensor_scalar(
                    h2b[:, mc * 256:mc * 256 + 256], GTMP[:, 0:256],
                    SM[l]["lngT"][:, mc:mc + 1], SM[l]["lnbT"][:, mc:mc + 1],
                    ALU.mult, ALU.add),
                   r=["GTMP0", f"SM{l}lngT", f"SM{l}lnbT"], w=[rH],
                   tag="F2 ap3")

        def f2_w1(c):
            h2b = H2B[c % 2]
            rH = f"H2B_{c % 2}"
            for j in range(22):
                M = 128 if j < 21 else 42
                ca = 128 * j
                cg = 2730 + 128 * j
                pa = PS[:, 0 + 512 * (j % 2):256 + 512 * (j % 2)]
                pg = PS[:, 1024 + 512 * (j % 2):1280 + 512 * (j % 2)]
                for mc in range(8):
                    lhsa = W1S[:, mc * 5460 + ca:mc * 5460 + ca + M]
                    lhsg = W1S[:, mc * 5460 + cg:mc * 5460 + cg + M]
                    rhs = h2b[:, mc * 256:mc * 256 + 256]
                    op("tensor", lambda h, pa=pa, l_=lhsa, r_=rhs, M=M,
                              st=(mc == 0), sp=(mc == 7):
                       h.matmul(pa[0:M, :], l_, r_, start=st, stop=sp),
                       r=[rH, "W1S"], w=psb(512 * (j % 2), 512 * (j % 2) + 256),
                       tag="F2 w1a")
                    op("tensor", lambda h, pg=pg, l_=lhsg, r_=rhs, M=M,
                              st=(mc == 0), sp=(mc == 7):
                       h.matmul(pg[0:M, :], l_, r_, start=st, stop=sp),
                       r=[rH, "W1S"],
                       w=psb(1024 + 512 * (j % 2), 1024 + 512 * (j % 2) + 256),
                       tag="F2 w1g")
                op("scalar", lambda h, M=M, j=j, l=l, pg=pg: h.activation(
                    GTMP[0:M, 256 * (j % 2):256 * (j % 2) + 256], pg[0:M, :],
                    AF.Gelu,
                    bias=SM[l]["b1g"][0:M, j:j + 1]),
                   r=psb(1024 + 512 * (j % 2), 1024 + 512 * (j % 2) + 256)
                   + [f"SM{l}b1g"], w=[f"GTMP{j % 2}"], tag="F2 gelu")
                op("vector", lambda h, M=M, j=j, l=l, pa=pa:
                   h.scalar_tensor_tensor(
                    ACTB[0:M, j * 256:j * 256 + 256], pa[0:M, :],
                    SM[l]["b1a"][0:M, j:j + 1],
                    GTMP[0:M, 256 * (j % 2):256 * (j % 2) + 256],
                    ALU.add, ALU.mult),
                   r=psb(512 * (j % 2), 512 * (j % 2) + 256)
                   + [f"GTMP{j % 2}", f"SM{l}b1a"], w=["ACTB"],
                   tag="F2 act")

        def f2_w2(c):
            xc2 = XC2[c % 2]
            rX = f"XC2_{c % 2}"
            for ct in range(8):
                pwc0 = 2048 + 512 * (ct % 2)
                pw = PS[:, pwc0:pwc0 + 256]
                for jn in range(22):
                    M = 128 if jn < 21 else 42
                    lhs = W2S[0:M, jn * 1024 + ct * 128:jn * 1024 + ct * 128 + 128]
                    rhs = ACTB[0:M, jn * 256:jn * 256 + 256]
                    op("tensor", lambda h, pw=pw, l_=lhs, r_=rhs, st=(jn == 0),
                              sp=(jn == 21):
                       h.matmul(pw, l_, r_, start=st, stop=sp),
                       r=["ACTB", "W2S"], w=psb(pwc0, pwc0 + 256), tag="F2 w2")
                x3 = X3T[ct % 2]
                rX3 = f"X3T{ct % 2}"
                op("vector", lambda h, ct=ct, x3=x3, l=l, pw=pw, xc2=xc2:
                   h.scalar_tensor_tensor(
                    x3[:], pw, SM[l]["b2T"][:, ct:ct + 1],
                    xc2[:, ct * 257 + 1:ct * 257 + 257], ALU.add, ALU.add),
                   r=psb(pwc0, pwc0 + 256) + [f"SM{l}b2T", rX], w=[rX3],
                   tag="F2 res")
                dma("scalar", x_out[ct * 128:ct * 128 + 128,
                                    256 * c:256 * c + 256], x3[:],
                    r=[rX3], w=[f"hbm_f2_{ct}_{c}"], tag="F2 out")

        f2_front(0)
        for c in range(16):
            f2_w1(c)
            if c < 15:
                f2_front(c + 1)
            f2_w2(c)

    P.fence()
    stage_f1(0, xS["xA"], xS["xB"])
    P.fence()
    stage_f2(0, xS["xB"], xS["xC"])
    P.fence()
    stage_f1(1, xS["xC"], xS["xD"])
    P.fence()
    stage_f2(1, xS["xD"], xS["xE"])
    P.fence()

    for c in range(8):
        s1 = xS["xA"].rearrange("(m p) t -> p m t", p=128)[
            :, :, 512 * c:512 * c + 512]
        s2 = xS["xE"].rearrange("(m p) t -> p m t", p=128)[
            :, :, 512 * c:512 * c + 512]
        dma("sync", XF1[:].rearrange("p (m t) -> p m t", t=512), s1,
            w=["XF1"], tag="FL load1")
        dma("sync", XF2[:].rearrange("p (m t) -> p m t", t=512), s2,
            w=["XF2"], tag="FL load2")
        op("vector", lambda h: h.tensor_tensor(XF1[:], XF1[:], XF2[:], ALU.add),
           r=["XF1", "XF2"], w=["XF1"], tag="FL add")

        def fl_apply(mc, rmub, rrsb, c=c):
            ot = OUTT[mc % 2]
            rO = f"OUTT{mc % 2}"
            op("vector", lambda h, mc=mc, ot=ot: h.tensor_tensor(
                ot[:], XF1[:, mc * 512:mc * 512 + 512], MUB[:], ALU.subtract),
               r=["XF1", rmub], w=[rO], tag="FL ap1")
            op("vector", lambda h, ot=ot: h.tensor_tensor(
                ot[:], ot[:], RSB[:], ALU.mult),
               r=[rO, rrsb], w=[rO], tag="FL ap2")
            dma("scalar", out_h[mc * 128:mc * 128 + 128, 512 * c:512 * c + 512],
                ot[:], r=[rO], w=[f"hbm_out_{mc}_{c}"], tag="FL out")

        ln_channel(512, lambda mc: XF1[:, mc * 512:mc * 512 + 512], ["XF1"],
                   RBT, R2T, MUB, RSB, fl_apply, tagp=f"flln{c}")

    stats = P.emit()
    return nc, stats


def _host_consts():
    DIM_HEAD, WINDOW = 64, 512
    half = np.arange(0, DIM_HEAD, 2, dtype=np.float64)
    inv_freq = 1.0 / 10000.0 ** (half / DIM_HEAD)
    t = np.arange(L, dtype=np.float64)
    freqs = t[:, None] * inv_freq[None, :]
    freqs = np.concatenate([freqs, freqs], -1)
    base = (half + 0.4 * DIM_HEAD) / (1.4 * DIM_HEAD)
    power = (t - L // 2) / float(WINDOW // 2)
    scale = base[None, :] ** power[:, None]
    scale = np.concatenate([scale, scale], -1)
    cos, sin = np.cos(freqs), np.sin(freqs)

    def arr(x):
        return np.ascontiguousarray(
            x.astype(np.float32).reshape(32, 128, 64).transpose(1, 0, 2)
            .reshape(128, 2048))

    tabs = dict(cs=arr(cos * scale), ss=arr(sin * scale),
                ci=arr(cos / scale), si=arr(sin / scale))
    identf = np.eye(128, dtype=np.float32)
    mtri = np.triu(np.ones((128, 128), np.float32), k=1) * np.float32(NEG)
    return tabs, identf, mtri


def _colmajor8(v):
    return np.ascontiguousarray(v.reshape(8, 128).T.astype(np.float32))


def _b1cols(v):
    w = np.zeros(22 * 128, np.float32)
    w[:2730] = v
    return np.ascontiguousarray(w.reshape(22, 128).T)


def kernel(L0_attn_x, L0_attn_Wq, L0_attn_bq, L0_attn_Wk, L0_attn_bk,
           L0_attn_Wv, L0_attn_bv, L0_attn_Wo, L0_attn_bo,
           L0_ffn_wqkv, L0_ffn_q_scale, L0_ffn_k_scale, L0_ffn_wo,
           L0_ffn_ln_g, L0_ffn_ln_b, L0_ffn_w1, L0_ffn_b1, L0_ffn_w2,
           L0_ffn_b2):
    f32 = np.float32
    x = np.asarray(L0_attn_x, f32)
    tabs, identf, mtri = _host_consts()
    common = dict(
        wq=np.ascontiguousarray(np.asarray(L0_attn_Wq, f32)),
        wk=np.ascontiguousarray(np.asarray(L0_attn_Wk, f32)),
        wv=np.ascontiguousarray(np.asarray(L0_attn_Wv, f32)),
        woa=np.ascontiguousarray(np.asarray(L0_attn_Wo, f32)),
        bq=np.asarray(L0_attn_bq, f32).reshape(1, D),
        bk=np.asarray(L0_attn_bk, f32).reshape(1, D),
        bvT=_colmajor8(np.asarray(L0_attn_bv, f32)),
        boT=_colmajor8(np.asarray(L0_attn_bo, f32)),
        identf=identf, mtri=mtri,
    )
    for t in ("cs", "ss", "ci", "si"):
        common[f"tab_{t}"] = tabs[t]
    for l in (0, 1):
        common[f"wqkv{l}"] = np.ascontiguousarray(np.asarray(L0_ffn_wqkv[l], f32))
        common[f"wof{l}"] = np.ascontiguousarray(np.asarray(L0_ffn_wo[l], f32))
        common[f"w1_{l}"] = np.ascontiguousarray(np.asarray(L0_ffn_w1[l], f32))
        common[f"w2_{l}"] = np.ascontiguousarray(np.asarray(L0_ffn_w2[l], f32))
        b1 = np.asarray(L0_ffn_b1[l], f32)
        common[f"b1a{l}"] = _b1cols(b1[:2730])
        common[f"b1g{l}"] = _b1cols(b1[2730:])
        common[f"b2T{l}"] = _colmajor8(np.asarray(L0_ffn_b2[l], f32))
        common[f"lngT{l}"] = _colmajor8(np.asarray(L0_ffn_ln_g[l], f32))
        common[f"lnbT{l}"] = _colmajor8(np.asarray(L0_ffn_ln_b[l], f32))
        common[f"qsrep{l}"] = np.ascontiguousarray(
            np.tile(np.asarray(L0_ffn_q_scale[l], f32), 32).reshape(1, 2048))
        common[f"ksrep{l}"] = np.ascontiguousarray(
            np.tile(np.asarray(L0_ffn_k_scale[l], f32), 32).reshape(1, 2048))

    nc, _ = build_nc(debug=False)
    in_maps = []
    for b in range(NCORE):
        m = dict(common)
        m["xT"] = np.ascontiguousarray(x[b].T)
        in_maps.append(m)
    res = run_bass_kernel_spmd(nc, in_maps, core_ids=list(range(NCORE)))
    out = np.stack([np.ascontiguousarray(r["out"].T) for r in res.results])
    return out


# version 30
# speedup vs baseline: 1.0651x; 1.0651x over previous
import numpy as np
import ml_dtypes
import concourse.bass as bass
import concourse.mybir as mybir
from concourse.bass_utils import run_bass_kernel_spmd

F32 = mybir.dt.float32
BF16 = mybir.dt.bfloat16
AF = mybir.ActivationFunctionType
ALU = mybir.AluOpType
AX = mybir.AxisListType

NCORE = 8
D = 1024
L = 4096
NEG = -1.0e30
EPS = 1e-5

ENGINES = ("tensor", "vector", "scalar", "gpsimd", "sync")
SELF_SYNC = {"vector", "scalar", "gpsimd", "sync"}


class Prog:

    def __init__(self, nc):
        self.nc = nc
        self.items = []
        self.cseq = {e: 0 for e in ENGINES}
        self.dseq = {e: 0 for e in ENGINES}
        self.last_w = {}
        self.readers = {}

    def op(self, eng, emit, r=(), w=(), dma=False, tag=""):
        deps = list(self._take_fence(eng))
        for res in r:
            p = self.last_w.get(res)
            if p is not None:
                deps.append(p)
        for res in w:
            p = self.last_w.get(res)
            if p is not None:
                deps.append(p)
            deps.extend(self.readers.get(res, ()))
        if dma:
            self.dseq[eng] += 1
            point = ("d", eng, self.dseq[eng])
        else:
            self.cseq[eng] += 1
            point = ("c", eng, self.cseq[eng])
        for res in w:
            self.last_w[res] = point
            self.readers[res] = []
        for res in r:
            self.readers.setdefault(res, []).append(point)
        best = {}
        for (k, e, s) in deps:
            if best.get((k, e), -1) < s:
                best[(k, e)] = s
        deps = [(k, e, s) for (k, e), s in best.items()]
        self.items.append((eng, emit, deps, point, dma, tag))
        return point

    def fence(self):
        pts = []
        for e in ENGINES:
            if self.cseq[e]:
                pts.append(("c", e, self.cseq[e]))
            if self.dseq[e]:
                pts.append(("d", e, self.dseq[e]))
        self._pending_fence = {e: list(pts) for e in ENGINES}

    def _take_fence(self, eng):
        pf = getattr(self, "_pending_fence", None)
        if pf and pf.get(eng):
            deps = pf[eng]
            pf[eng] = []
            return deps
        return []

    def emit(self):
        nc = self.nc
        csem = {}
        dsem = {}
        for e in ENGINES:
            if self.cseq[e] > 0:
                csem[e] = nc.semaphore(f"c_{e}").__enter__()
            if self.dseq[e] > 0:
                dsem[e] = nc.semaphore(f"d_{e}").__enter__()
        plans = {e: [] for e in ENGINES}
        for (eng, emit, deps, point, dma, tag) in self.items:
            waits = []
            for (k, pe, s2) in deps:
                if k == "c" and pe == eng and eng not in SELF_SYNC:
                    continue
                waits.append((k, pe, s2))
            if dma and point[2] > 1:
                waits.append(("d", eng, point[2] - 1))
            plans[eng].append((waits, emit, point, dma, tag))

        runend = {e: {} for e in ENGINES}
        for e in ENGINES:
            cur_run = []
            for (waits, _em, point, dma, _t) in plans[e]:
                if waits and cur_run:
                    for s2 in cur_run:
                        runend[e][s2] = cur_run[-1]
                    cur_run = []
                if dma:
                    cur_run.append(point[2])
            for s2 in cur_run:
                runend[e][s2] = cur_run[-1]

        water = {e: {} for e in ENGINES}
        final = {e: [] for e in ENGINES}
        for e in ENGINES:
            for (waits, emitf, point, dma, tag) in plans[e]:
                rw = []
                for (k, pe, s2) in waits:
                    if k == "c":
                        need, sem = s2, csem[pe]
                    else:
                        need, sem = 16 * runend[pe].get(s2, s2), dsem[pe]
                    if water[e].get((k, pe), 0) >= need:
                        continue
                    water[e][(k, pe)] = need
                    rw.append((sem, need))
                inc = (dsem[e], 16) if dma else (csem[e], 1)
                final[e].append((rw, emitf, inc, tag))

        with nc.Block() as blk:
            def make(eng):
                def run(handle):
                    for rw, emitf, (sem, n), tag in final[eng]:
                        for (wsem, val) in rw:
                            handle.wait_ge(wsem, val)
                        emitf(handle).then_inc(sem, n)
                    if eng in dsem:
                        handle.wait_ge(dsem[eng], 16 * self.dseq[eng])
                return run
            for e in ENGINES:
                if final[e]:
                    getattr(blk, e)(make(e))
        return {"csem": {e: self.cseq[e] for e in ENGINES},
                "dsem": {e: self.dseq[e] for e in ENGINES},
                "n_ops": len(self.items)}


def psb(c0, c1):
    return [f"psbank{i}" for i in range(c0 // 512, (c1 + 511) // 512)]


def build_nc(debug=False):
    nc = bass.Bass("TRN2", target_bir_lowering=False)
    P = Prog(nc)
    op = P.op

    def inp(name, shape, dt=F32):
        return nc.dram_tensor(name, list(shape), dt, kind="ExternalInput").ap()

    xT_h = inp("xT", (D, L))
    wq_h = inp("wq", (D, D)); wk_h = inp("wk", (D, D)); wv_h = inp("wv", (D, D))
    woa_h = inp("woa", (D, D))
    bq_h = inp("bq", (1, D)); bk_h = inp("bk", (1, D))
    bvT_h = inp("bvT", (128, 8)); boT_h = inp("boT", (128, 8))
    identf_h = inp("identf", (128, 128))
    mtri_h = inp("mtri", (128, 128))
    tab_h = {t: inp(f"tab_{t}", (128, 2048)) for t in ("cs", "ss", "ci", "si")}
    ffn_h = {}
    for l in (0, 1):
        ffn_h[l] = dict(
            wqkv=inp(f"wqkv{l}", (D, 1536)), wof=inp(f"wof{l}", (512, D)),
            w1=inp(f"w1_{l}", (D, 5460)), w2=inp(f"w2_{l}", (2730, D)),
            b1a=inp(f"b1a{l}", (128, 22)), b1g=inp(f"b1g{l}", (128, 22)),
            b2T=inp(f"b2T{l}", (128, 8)), lngT=inp(f"lngT{l}", (128, 8)),
            lnbT=inp(f"lnbT{l}", (128, 8)),
            qsrep=inp(f"qsrep{l}", (1, 2048)), ksrep=inp(f"ksrep{l}", (1, 2048)),
        )
    okind = "ExternalOutput" if debug else "Internal"
    xS = {k: nc.dram_tensor(k, [D, L], F32, kind=okind).ap()
          for k in ("xA", "xB", "xC", "xD", "xE")}
    out_h = nc.dram_tensor("out", [D, L], F32, kind="ExternalOutput").ap()

    PS = nc.alloc_psum_tensor("PS", [128, 4096], F32)
    PSH = PS[:].bitcast(BF16)

    sba = nc.alloc_sbuf_tensor
    IDBF = sba("IDBF", [128, 128], BF16)
    MTRI = sba("MTRI", [128, 128], BF16)
    ONESB = sba("ONESB", [1, 128], BF16)
    ONECB = sba("ONECB", [128, 1], BF16)
    ONES64 = sba("ONES64", [65, 128], F32)
    CEPS = sba("CEPS", [1, 1], F32)
    nc.const_aps.register(F32, EPS, CEPS[:])
    ONESF = sba("ONESF", [1, 128], F32)
    EIGHTF = sba("EIGHTF", [1, 128], F32)
    BQ = sba("BQ", [1, D], BF16)
    BK = sba("BK", [1, D], BF16)
    BVT = sba("BVT", [128, 8], F32)
    BOT = sba("BOT", [128, 8], F32)
    SM = {}
    for l in (0, 1):
        SM[l] = dict(
            b1a=sba(f"B1A{l}", [128, 22], F32), b1g=sba(f"B1G{l}", [128, 22], F32),
            b2T=sba(f"B2T{l}", [128, 8], F32), lngT=sba(f"LNG{l}", [128, 8], F32),
            lnbT=sba(f"LNB{l}", [128, 8], F32),
        )
    STATC = sba("STATC", [128, 64], F32)
    NMX = sba("NMX", [128, 16], F32)
    ROW = sba("ROWB", [1, 2048], F32)

    assert nc.sbuf_base < 33 * 1024, f"bump region overflow {nc.sbuf_base}"
    LIMIT = 229345
    R0 = 33 * 1024
    cur = [R0]

    def nxt(name, shape, dt, at=None):
        offi = cur[0] if at is None else at
        t = nc.alloc_sbuf_tensor_at(name, list(shape), dt, offset=offi)
        if at is None:
            nb = shape[1] * (2 if dt == BF16 else 4)
            cur[0] = offi + (nb + 31) // 32 * 32
        return t

    WQ = nxt("WQ", (128, 8 * D), BF16)
    WK = nxt("WK", (128, 8 * D), BF16)
    WV = nxt("WV", (128, 8 * D), BF16)
    WOA = nxt("WOA", (128, 8 * D), BF16)
    XC = [nxt("XCs", (128, 4096), F32)] * 2
    XR = [nxt("XRs", (128, 512), F32)] * 2
    XPT = [nxt("XPT0", (128, 4096), BF16)] * 2
    QV = [nxt(f"QV{i}", (128, 4096), BF16) for i in (0, 1)]
    KV = [nxt(f"KV{i}", (128, 4096), BF16) for i in (0, 1)]
    VVT = [nxt(f"VVT{i}", (128, 2 * 2080), BF16) for i in (0, 1)]
    P2 = [nxt(f"P2_{i}", (128, 1024), BF16) for i in (0, 1)]
    ATOK = nxt("ATOK", (128, 256), BF16)
    ANB = [nxt(f"ANB{i}", (128, 4096), BF16) for i in (0, 1)]
    UNS = nxt("UNS", (128, 4096), BF16)
    RT = nxt("RT", (128, 4096), F32)
    RBT = nxt("RBT", (128, 512), BF16)
    R2T = nxt("R2T", (128, 512), BF16)
    X1T = [nxt("X1Ts", (128, 512), F32)] * 2
    MUB = nxt("MUB", (128, 512), F32)
    RSB = nxt("RSB", (128, 512), F32)
    S1END = cur[0]
    assert S1END < LIMIT, f"S1 region overflow {S1END}"

    cur[0] = R0
    WQKV = nxt("WQKV", (128, 8 * 1536), BF16)
    WOF = nxt("WOF", (128, 4 * 1024), BF16)
    XW = nxt("XW", (128, 8 * 513), F32)
    HT = nxt("HT", (128, 8 * 512), BF16)
    QF = nxt("QF", (128, 2048), F32)
    KF = nxt("KF", (128, 2048), F32)
    QR = nxt("QR", (128, 2048), BF16)
    KR = nxt("KR", (128, 2048), BF16)
    VRA = nxt("VRA", (128, 2048), BF16)
    VRB = nxt("VRB", (128, 2048), BF16)
    qhb_off0 = cur[0]
    QHB = nxt("QHB", (128, 2048), BF16)
    KHB = nxt("KHB", (128, 2048), BF16)
    QT2 = [nxt(f"QT2_{i}", (128, 2048), BF16) for i in (0, 1)]
    KT2 = [nxt(f"KT2_{i}", (128, 2048), BF16) for i in range(3)]
    VRING = [nxt(f"VRING{i}", (128, 4 * 520), BF16) for i in range(3)]
    PB = [nxt(f"PB{i}", (128, 640), BF16) for i in range(4)]
    PTB = [nxt(f"PTB{i}", (128, 640), BF16) for i in range(2)]
    ow_off0 = cur[0]
    OW = nxt("OW", (128, 2048), BF16)
    OT = nxt("OT", (128, 2048), BF16)
    X2T = [nxt(f"X2T{i}", (128, 512), F32) for i in (0, 1)]
    XR2 = [nxt(f"XR2_{i}", (128, 512), F32) for i in (0, 1)]
    INB = [nxt(f"INB{i}", (128, 64), F32) for i in (0, 1)]
    T1Q = nxt("T1Q", (128, 2048), BF16)
    T2Q = nxt("T2Q", (128, 2048), BF16)
    T1K = nxt("T1K", (128, 2048), BF16)
    T2K = nxt("T2K", (128, 2048), BF16)
    TRAW = nxt("TRAW", (128, 2048), F32, at=qhb_off0)
    QSB = nxt("QSB", (128, 2048), F32, at=ow_off0)
    F1END = cur[0]
    assert F1END < LIMIT, f"F1 region overflow {F1END}"

    cur[0] = R0
    W1S = nxt("W1S", (128, 8 * 5460), BF16)
    W2S = nxt("W2S", (128, 22 * 1024), BF16)
    XC2 = [nxt(f"XC2_{i}", (128, 8 * 257), F32) for i in (0, 1)]
    H2B = [nxt(f"H2B_{i}", (128, 8 * 256), BF16) for i in (0, 1)]
    ACTB = nxt("ACTB", (128, 22 * 256), BF16)
    GTMP = nxt("GTMP", (128, 512), F32)
    R2C = nxt("R2C", (128, 256), BF16)
    X3T = [nxt(f"X3T{i}", (128, 256), F32) for i in (0, 1)]
    MUB2 = nxt("MUB2", (128, 256), F32)
    RSB2 = nxt("RSB2", (128, 256), F32)
    F2END = cur[0]
    assert F2END < LIMIT, f"F2 region overflow {F2END}"

    cur[0] = R0
    XF1 = [nxt(f"XF1_{i}", (128, 4096), F32) for i in (0, 1)]
    XF2 = [nxt(f"XF2_{i}", (128, 4096), F32) for i in (0, 1)]
    OUTT = [nxt(f"OUTT{i}", (128, 512), F32) for i in (0, 1)]
    FLEND = cur[0]
    assert FLEND < LIMIT

    dbg_bufs = {}

    def dbg(name, ap, deps):
        if not debug:
            return
        dt = nc.dram_tensor("dbg_" + name, list(ap.shape[:2]) if len(ap.shape) == 2
                            else list(ap.shape), ap.dtype, kind="ExternalOutput").ap()
        op("scalar", lambda h, d=dt, s2=ap: h.dma_start(d, s2),
           r=deps, w=[f"dbg{name}"], dma=True, tag="dbg " + name)

    def dma(eng, dst, src, r=(), w=(), tag=""):
        return op(eng, lambda h, d=dst, s=src: h.dma_start(d, s),
                  r=r, w=w, dma=True, tag=tag)

    def dma_cast(dst, src, r=(), w=(), tag=""):
        return op("gpsimd", lambda h, d=dst, s=src: h.dma_start(d, s),
                  r=r, w=w, dma=True, tag=tag)

    dma_cast(IDBF[:], identf_h, w=["IDBF"])
    dma_cast(MTRI[:], mtri_h, w=["MTRI"])
    dma_cast(BQ[:], bq_h, w=["BQb"])
    dma_cast(BK[:], bk_h, w=["BKb"])
    dma("sync", BVT[:], bvT_h, w=["BVT"])
    dma("sync", BOT[:], boT_h, w=["BOT"])
    for l in (0, 1):
        for k2 in ("b1a", "b1g", "b2T", "lngT", "lnbT"):
            dma("sync", SM[l][k2][:], ffn_h[l][k2], w=[f"SM{l}{k2}"])
    op("vector", lambda h: h.memset(ONESB[:], 1.0), w=["ONESB"])
    op("vector", lambda h: h.memset(ONECB[:], 1.0), w=["ONECB"])
    op("vector", lambda h: h.memset(ONES64[64:65, :], 1.0), w=["ONES64"])
    op("vector", lambda h: h.memset(CEPS[:], EPS), w=["CEPS"])
    op("vector", lambda h: h.memset(ONESF[:], 1.0), w=["ONESF"])
    op("vector", lambda h: h.memset(EIGHTF[:], 8.0), w=["EIGHTF"])

    def wload(dst, src, resname, cols, nchunk=8):
        s3 = src.rearrange("(m p) c -> p m c", p=128)
        dma_cast(dst[:].rearrange("p (m c) -> p m c", c=cols), s3,
                 w=[resname], tag=f"load {resname}")

    wload(WQ, wq_h, "WQ", D)
    wload(WK, wk_h, "WK", D)
    wload(WV, wv_h, "WV", D)
    wload(WOA, woa_h, "WOA", D)

    def ln_channel(T, read_tile, rsrc, rb, r2, mub, rsb, apply_cb,
                   c1=2048, c2=2560, tagp="ln"):
        rrb = tagp + "_rb"
        rr2 = tagp + "_r2"
        for mc in range(8):
            src = read_tile(mc)
            op("vector", lambda h, s=src: h.tensor_copy(rb[:, 0:T], s),
               r=rsrc, w=[rrb], tag=tagp + " rbcast")
            op("vector", lambda h: h.tensor_tensor(r2[:, 0:T], rb[:, 0:T],
                                                   rb[:, 0:T], ALU.mult),
               r=[rrb], w=[rr2], tag=tagp + " r2")
            op("tensor",
               lambda h, st=(mc == 0), sp=(mc == 7): h.matmul(
                   PS[0:1, c1:c1 + T], ONECB[:], rb[:, 0:T],
                   start=st, stop=sp),
               r=[rrb, "ONECB"], w=[tagp + "_s1"] + psb(c1, c1 + T),
               tag=tagp + " s1")
            op("tensor",
               lambda h, st=(mc == 0), sp=(mc == 7): h.matmul(
                   PS[0:1, c2:c2 + T], ONECB[:], r2[:, 0:T],
                   start=st, stop=sp),
               r=[rr2, "ONECB"], w=[tagp + "_s2"] + psb(c2, c2 + T),
               tag=tagp + " s2")
        op("vector", lambda h: h.tensor_scalar(
            ROW[0:1, 0:T], PS[0:1, c1:c1 + T], 1.0 / D, None, ALU.mult),
           r=[tagp + "_s1"] + psb(c1, c1 + T), w=[tagp + "_mu"], tag=tagp + " mu")
        op("vector", lambda h: h.tensor_scalar(
            ROW[0:1, 512:512 + T], PS[0:1, c2:c2 + T], 1.0 / D, None, ALU.mult),
           r=[tagp + "_s2"] + psb(c2, c2 + T), w=[tagp + "_m2"], tag=tagp + " m2")
        op("vector", lambda h: h.tensor_tensor(
            ROW[0:1, 1024:1024 + T], ROW[0:1, 0:T], ROW[0:1, 0:T], ALU.mult),
           r=[tagp + "_mu"], w=[tagp + "_v"], tag=tagp + " mu2")
        op("vector", lambda h: h.tensor_tensor(
            ROW[0:1, 1024:1024 + T], ROW[0:1, 512:512 + T],
            ROW[0:1, 1024:1024 + T], ALU.subtract),
           r=[tagp + "_m2", tagp + "_v"], w=[tagp + "_v"], tag=tagp + " var")
        op("scalar", lambda h: h.activation(
            ROW[0:1, 1024:1024 + T], ROW[0:1, 1024:1024 + T], AF.Ln, bias=EPS),
           r=[tagp + "_v", "CEPS"], w=[tagp + "_v"], tag=tagp + " ln")
        op("scalar", lambda h: h.activation(
            ROW[0:1, 1024:1024 + T], ROW[0:1, 1024:1024 + T], AF.Exp, scale=-0.5),
           r=[tagp + "_v"], w=[tagp + "_v"], tag=tagp + " rstd")
        op("tensor", lambda h: h.matmul(
            PS[:, c1:c1 + T], ONESF[:, 0:128], ROW[0:1, 0:T],
            start=True, stop=True),
           r=[tagp + "_mu"], w=[tagp + "_bc1"] + psb(c1, c1 + T), tag=tagp + " bc1")
        op("vector", lambda h: h.tensor_copy(mub[:, 0:T], PS[:, c1:c1 + T]),
           r=[tagp + "_bc1"] + psb(c1, c1 + T), w=[tagp + "_mub"], tag=tagp)
        op("tensor", lambda h: h.matmul(
            PS[:, c2:c2 + T], ONESF[:, 0:128], ROW[0:1, 1024:1024 + T],
            start=True, stop=True),
           r=[tagp + "_v"], w=[tagp + "_bc2"] + psb(c2, c2 + T), tag=tagp + " bc2")
        op("vector", lambda h: h.tensor_copy(rsb[:, 0:T], PS[:, c2:c2 + T]),
           r=[tagp + "_bc2"] + psb(c2, c2 + T), w=[tagp + "_rsb"], tag=tagp)
        for mc in range(8):
            apply_cb(mc, tagp + "_mub", tagp + "_rsb")

    def s1_load(cp):
        xc = XC[cp % 2]
        rXC = "XCs"
        xpt = XPT[cp % 2]
        rXP = "XPT0"
        src = xT_h.rearrange("(m p) t -> p m t", p=128)[:, :, 512 * cp:512 * cp + 512]
        dma("sync", xc[:].rearrange("p (m t) -> p m t", t=512), src,
            w=[rXC], tag=f"S1 load {cp}")
        for mc in range(8):
            for half in range(2):
                cb = mc * 512 + half * 256
                sap = xc[:, cb:cb + 256].rearrange("p (j t) -> p t j", t=4)
                dap = xpt[:, cb:cb + 256].rearrange("p (t j) -> p t j", j=64)
                op("gpsimd", lambda h, d=dap, s=sap: h.tensor_copy(d, s),
                   r=[rXC], w=[rXP], tag="S1 perm")

    def s1_qkv_groups(cp):
        xpt = XPT[cp % 2]
        rXP = "XPT0"
        for (W, res, BR, bres, dstl, dres) in (
            (WQ, "WQ", BQ, "BQb", QV, "QV"), (WK, "WK", BK, "BKb", KV, "KV"),
        ):
            for tau in range(4):
                for nh in range(2):
                    def emit(W=W, res=res, BR=BR, bres=bres, dstl=dstl,
                             dres=dres, tau=tau, nh=nh, xpt=xpt, rXP=rXP, cp=cp):
                        qc0 = 0 if (tau * 2 + nh) % 2 == 0 else 2048
                        pq = PS[:, qc0:qc0 + 512]
                        for mc in range(8):
                            l0 = xpt[:, mc * 512 + tau * 64:
                                     mc * 512 + tau * 64 + 64]
                            l1 = xpt[:, mc * 512 + 256 + tau * 64:
                                     mc * 512 + 256 + tau * 64 + 64]
                            rhs = W[:, mc * 1024 + nh * 512:
                                    mc * 1024 + nh * 512 + 512]
                            op("tensor",
                               lambda h, o_=pq, l0=l0, l1=l1, r_=rhs,
                                      st=(mc == 0):
                               (h.matmul(o_[0:64, :], l0, r_, start=st,
                                         stop=False, tile_position=(0, 0)),
                                h.matmul(o_[64:128, :], l1, r_, start=st,
                                         stop=False, tile_position=(0, 64)))[-1],
                               r=[rXP, res], w=psb(qc0, qc0 + 512),
                               tag=f"S1 {res} mm")
                        br = BR[:, nh * 512:nh * 512 + 512]
                        op("tensor",
                           lambda h, o_=pq, b=br:
                           (h.matmul(o_[0:64, :], ONESB[0:1, 0:64], b,
                                     start=False, stop=True,
                                     tile_position=(0, 0)),
                            h.matmul(o_[64:128, :], ONESB[0:1, 0:64], b,
                                     start=False, stop=True,
                                     tile_position=(0, 64)))[-1],
                           r=["ONESB", bres], w=psb(qc0, qc0 + 512),
                           tag="S1 bias")
                        op("vector",
                           lambda h, d=dstl[cp % 2], t_=tau, nh=nh, qc0=qc0:
                           h.tensor_copy(
                               d[:, t_ * 1024 + nh * 512:
                                 t_ * 1024 + nh * 512 + 512],
                               PS[:, qc0:qc0 + 512]),
                           r=psb(qc0, qc0 + 512), w=[dres + str(cp % 2)],
                           tag="S1 qkevac")
                    yield emit

    def s1_vt_groups(cp):
        xpt = XPT[cp % 2]
        rXP = "XPT0"
        vvt = VVT[cp % 2]
        rVV = f"VVT{cp % 2}"
        def emit_ones(vvt=vvt, rVV=rVV):
            op("vector",
               lambda h: h.memset(
                   vvt[:].rearrange("p (b r) -> p b r", r=65)[:, :, 64:65], 1.0),
               w=[rVV], tag="S1 vvt ones")
        yield emit_ones
        for g in range(2):
            for kap in range(8):
                def emit(g=g, kap=kap, xpt=xpt, rXP=rXP, vvt=vvt, rVV=rVV):
                    vc0 = 1024 if (g * 8 + kap) % 2 == 0 else 2560
                    pv = PS[:, vc0:vc0 + 256]
                    for mc in range(8):
                        lhs = WV[:, mc * 1024 + kap * 128:
                                 mc * 1024 + kap * 128 + 128]
                        rhs = xpt[:, mc * 512 + g * 256:mc * 512 + g * 256 + 256]
                        op("tensor",
                           lambda h, l_=lhs, r_=rhs, st=(mc == 0), sp=(mc == 7):
                           h.matmul(pv, l_, r_, start=st, stop=sp),
                           r=[rXP, "WV"], w=psb(vc0, vc0 + 256), tag="S1 vt mm")
                    dstap = vvt[:, g * 2080:g * 2080 + 2080].rearrange(
                        "p (t k r) -> p t k r", k=8, r=65)[:, :, kap, 0:64]
                    sap = pv.rearrange("p (t d2) -> p t d2", d2=64)
                    op("vector",
                       lambda h, d=dstap, s=sap, kap=kap: h.tensor_scalar(
                           d, s, BVT[:, kap:kap + 1], None, ALU.add),
                       r=psb(vc0, vc0 + 256) + ["BVT"], w=[rVV],
                       tag="S1 vt evac")
                yield emit

    def s1_attention(cp):
        qv = QV[cp % 2]
        kv = KV[cp % 2]
        vvt = VVT[cp % 2]
        rQ = f"QV{cp % 2}"
        rK = f"KV{cp % 2}"
        rVV = f"VVT{cp % 2}"

        def emit_S(kc, qb):
            c0 = 1024 * (1 + kc % 3)
            rS = psb(c0, c0 + 1024)
            for g in range(2):
                gb = 64 * g
                lhs = kv[gb:gb + 64, kc * 128:kc * 128 + 128]
                rhs = qv[gb:gb + 64, qb * 512:qb * 512 + 512]
                o_ = PS[:, c0 + g * 512:c0 + g * 512 + 512]
                op("tensor",
                   lambda h, o2=o_, l_=lhs, r_=rhs, gb=gb:
                   h.matmul(o2, l_, r_, start=True, stop=True,
                            tile_position=(gb, 0)),
                   r=[rK, rQ], w=psb(c0 + g * 512, c0 + g * 512 + 512),
                   tag="S1 S")
            c0w = 1024 * (1 + (kc + 1) % 3)
            op("tensor",
               lambda h, c0w=c0w: h.matmul(PS[:, c0w:c0w + 512], WQ[:, 0:128],
                                           WQ[:, 0:512], start=True, stop=True),
               r=["WQ"], w=psb(c0w, c0w + 512), tag="S1 warm")
            op("scalar",
               lambda h, d=P2[kc % 2], s=PS[:, c0:c0 + 1024]: h.activation(
                   d[:], s, AF.Exp, scale=1.0 / 1024.0),
               r=rS, w=[f"P2{kc % 2}"], tag="S1 exp")

        def emit_PV(kc):
            p2 = P2[kc % 2]
            rP = f"P2{kc % 2}"
            for g in range(2):
                ab = 512 * g
                for qs in range(4):
                    lhs = p2[:, g * 512 + qs * 128:g * 512 + qs * 128 + 128]
                    rhs = vvt[:, g * 2080 + kc * 65:g * 2080 + kc * 65 + 65]
                    op("tensor",
                       lambda h, l_=lhs, r_=rhs, ab=ab, qs=qs,
                              st=(kc == 0 and qs == 0), sp=(kc == 31 and qs == 3):
                       h.matmul(PS[:, ab + qs * 65:ab + qs * 65 + 65], l_, r_,
                                start=st, stop=sp),
                       r=[rP, rVV], w=psb(ab, ab + 260), tag="S1 PV")

        def emit_tail(qb):
            for g in range(2):
                ab = 512 * g
                gb = 64 * g
                rA = psb(ab, ab + 260)
                rc = PS[:, ab:ab + 260].rearrange(
                    "p (q e) -> p q e", e=65)[:, :, 64:65]
                op("vector", lambda h, rc=rc, g=g: h.reciprocal(
                    NMX[:, 8 + 4 * g:12 + 4 * g].rearrange(
                        "p (q e) -> p q e", e=1), rc),
                   r=rA, w=[f"NMXt{g}"], tag="S1 arcp")
                for qs in range(4):
                    op("vector", lambda h, qs=qs, ab=ab, g=g: h.tensor_scalar(
                        ATOK[:, qs * 64:qs * 64 + 64],
                        PS[:, ab + qs * 65:ab + qs * 65 + 64],
                        NMX[:, 8 + 4 * g + qs:9 + 4 * g + qs], None, ALU.mult),
                       r=rA + [f"NMXt{g}"], w=["ATOK"], tag="S1 anrm")
                for qs in range(4):
                    pt = PSH[0:64, 2 * ab + qs * 128:2 * ab + qs * 128 + 128]
                    op("tensor", lambda h, o2=pt, qs=qs:
                       h.transpose(o2, ATOK[:, qs * 64:qs * 64 + 64], IDBF[:]),
                       r=["ATOK", "IDBF"], w=rA, tag="S1 atr")
                src = PSH[0:64, 2 * ab:2 * ab + 512]
                op("vector",
                   lambda h, s2=src, gb=gb, qb=qb: h.tensor_copy(
                       ANB[cp % 2][gb:gb + 64, qb * 512:qb * 512 + 512], s2),
                   r=rA, w=[f"ANB{cp % 2}"], tag="S1 aev")

        state = {"pend": None}
        blocks = []
        for qb in range(8):
            def blk(qb=qb):
                for kc in range(32):
                    emit_S(kc, qb)
                    if kc == 1 and state["pend"] is not None:
                        emit_tail(state["pend"])
                        state["pend"] = None
                    if kc > 0:
                        emit_PV(kc - 1)
                emit_PV(31)
                state["pend"] = qb
            blocks.append(blk)
        blocks.append(lambda: emit_tail(state["pend"]))
        return blocks

    def s1_unscr(cp):
        for g in range(2):
            for p_ in range(16):
                kap = p_ // 2
                rb_ = 64 * (p_ % 2)
                sap = ANB[cp % 2][g * 64:g * 64 + 64, :].rearrange(
                    "p (t s) -> p t s", s=16)[:, :, p_:p_ + 1]
                op("vector",
                   lambda h, s=sap, kap=kap, rb2=rb_, g=g: h.tensor_copy(
                       UNS[rb2:rb2 + 64,
                           kap * 512 + g * 256:kap * 512 + g * 256 + 256]
                       .rearrange("p (t o) -> p t o", o=1), s),
                   r=[f"ANB{cp % 2}"], w=["UNS"], tag="S1 unscr")

    def s1_wo_groups(cp):
        for ct in range(8):
            def emit(ct=ct, cp=cp):
                xr = XR[0]
                rXR = "XRs"
                dma("sync", xr[:],
                    xT_h[ct * 128:ct * 128 + 128, 512 * cp:512 * cp + 512],
                    w=[rXR], tag="S1 xr")
                wc0 = 3072 if ct % 2 == 0 else 3584
                pw = PS[:, wc0:wc0 + 512]
                for nch in range(8):
                    lhs = WOA[:, nch * 1024 + ct * 128:
                              nch * 1024 + ct * 128 + 128]
                    rhs = UNS[:, nch * 512:nch * 512 + 512]
                    op("tensor",
                       lambda h, l_=lhs, r_=rhs, st=(nch == 0), sp=(nch == 7):
                       h.matmul(pw, l_, r_, start=st, stop=sp),
                       r=["UNS", "WOA"], w=psb(wc0, wc0 + 512), tag="S1 wo")
                op("vector",
                   lambda h, ct=ct, xr=xr, pw=pw: h.scalar_tensor_tensor(
                       RT[:, ct * 512:ct * 512 + 512], pw,
                       BOT[:, ct:ct + 1], xr[:],
                       ALU.add, ALU.add),
                   r=psb(wc0, wc0 + 512) + ["BOT", rXR], w=["RT"], tag="S1 res")
            yield emit

    def s1_ln(cp):
        def s1_apply(mc, rmub, rrsb, cp=cp):
            xt = X1T[0]
            rX1 = "X1Ts"
            op("vector",
               lambda h, mc=mc, xt=xt: h.tensor_tensor(
                   xt[:], RT[:, mc * 512:mc * 512 + 512], MUB[:], ALU.subtract),
               r=["RT", rmub], w=[rX1], tag="S1 lnap1")
            op("vector",
               lambda h, xt=xt: h.tensor_tensor(xt[:], xt[:], RSB[:], ALU.mult),
               r=[rX1, rrsb], w=[rX1], tag="S1 lnap2")
            dma("scalar", xS["xA"][mc * 128:mc * 128 + 128,
                                   512 * cp:512 * cp + 512], xt[:],
                r=[rX1], w=[f"hbm_xA_{mc}_{cp}"], tag="S1 out")

        ln_channel(512, lambda mc: RT[:, mc * 512:mc * 512 + 512], ["RT"],
                   RBT, R2T, MUB, RSB, s1_apply, tagp=f"s1ln{cp}")

    s1_load(0)
    for p in range(8):
        qkv = list(s1_qkv_groups(p))
        vt = list(s1_vt_groups(p))
        woq = list(s1_wo_groups(p - 1)) if p > 0 else []
        if p > 0:
            s1_unscr(p - 1)
        front = []
        for i in range(max(len(qkv), len(woq))):
            if i < len(qkv):
                front.append(qkv[i])
            if i < len(woq):
                front.append(woq[i])
        front.extend(vt)
        for f in front:
            f()
        if p < 7:
            s1_load(p + 1)
        if p > 0:
            s1_ln(p - 1)
        for b in s1_attention(p):
            b()
    s1_unscr(7)
    for f in s1_wo_groups(7):
        f()
    s1_ln(7)


    def build_tables(l):
        dma("sync", ROW[0:1, 0:2048], ffn_h[l]["qsrep"], w=["ROWq"], tag="qsrep")
        for j in range(4):
            op("tensor", lambda h, j=j: h.matmul(
                PS[:, 3072 + 0:3072 + 512], EIGHTF[0:1, 0:128],
                ROW[0:1, j * 512:j * 512 + 512], start=True, stop=True),
               r=["ROWq"], w=psb(3072, 3584), tag="qs bc")
            op("vector", lambda h, j=j: h.tensor_copy(
                QSB[:, j * 512:j * 512 + 512], PS[:, 3072:3584]),
               r=psb(3072, 3584), w=["QSB"], tag="qs ev")
        dma("sync", TRAW[:], tab_h["cs"], w=["TRAW"], tag="cs")
        op("vector", lambda h: h.tensor_tensor(T1Q[:], TRAW[:], QSB[:], ALU.mult),
           r=["TRAW", "QSB"], w=["T1Q"], tag="T1Q")
        qsb_hi = QSB[:].rearrange("p (t d) -> p t d", d=64)[:, :, 32:64]
        qsb_lo = QSB[:].rearrange("p (t d) -> p t d", d=64)[:, :, 0:32]
        dma("sync", TRAW[:], tab_h["ss"], r=["T1Q"], w=["TRAW"], tag="ss")

        def t2build(T2, rT2):
            t2_lo = T2[:].rearrange("p (t d) -> p t d", d=64)[:, :, 0:32]
            t2_hi = T2[:].rearrange("p (t d) -> p t d", d=64)[:, :, 32:64]
            ss_lo = TRAW[:].rearrange("p (t d) -> p t d", d=64)[:, :, 0:32]
            ss_hi = TRAW[:].rearrange("p (t d) -> p t d", d=64)[:, :, 32:64]
            op("vector", lambda h: h.scalar_tensor_tensor(
                t2_lo, ss_lo, -1.0, qsb_hi, ALU.mult, ALU.mult),
               r=["TRAW", "QSB"], w=[rT2], tag="T2lo")
            op("vector", lambda h: h.tensor_tensor(t2_hi, ss_hi, qsb_lo,
                                                   ALU.mult),
               r=["TRAW", "QSB"], w=[rT2], tag="T2hi")

        t2build(T2Q, "T2Q")
        dma("sync", ROW[0:1, 0:2048], ffn_h[l]["ksrep"], r=["ROWq"], w=["ROWq"],
            tag="ksrep")
        for j in range(4):
            op("tensor", lambda h, j=j: h.matmul(
                PS[:, 3072:3584], ONESF[0:1, 0:128],
                ROW[0:1, j * 512:j * 512 + 512], start=True, stop=True),
               r=["ROWq"], w=psb(3072, 3584), tag="ks bc")
            op("vector", lambda h, j=j: h.tensor_copy(
                QSB[:, j * 512:j * 512 + 512], PS[:, 3072:3584]),
               r=psb(3072, 3584), w=["QSB"], tag="ks ev")
        dma("sync", TRAW[:], tab_h["ci"], r=["T2Q"], w=["TRAW"], tag="ci")
        op("vector", lambda h: h.tensor_tensor(T1K[:], TRAW[:], QSB[:], ALU.mult),
           r=["TRAW", "QSB"], w=["T1K"], tag="T1K")
        dma("sync", TRAW[:], tab_h["si"], r=["T1K"], w=["TRAW"], tag="si")
        t2build(T2K, "T2K")
        if debug and l == 0:
            dbg("T1Q", T1Q[:], ["T1Q"])
            dbg("T2Q", T2Q[:], ["T2Q"])
            dbg("T1K", T1K[:], ["T1K"])
            dbg("T2K", T2K[:], ["T2K"])

    def stage_f1(l, x_in, x_out):
        wload(WQKV, ffn_h[l]["wqkv"], "WQKV", 1536)
        wof3 = ffn_h[l]["wof"].rearrange("(m p) c -> p m c", p=128)
        dma_cast(WOF[:].rearrange("p (m c) -> p m c", c=1024), wof3, w=["WOF"],
                 tag="load WOF")
        build_tables(l)

        def f1_load(w_):
            if w_ == 0:
                op("vector", lambda h: h.memset(
                    XW[:].rearrange("p (m t) -> p m t", t=513)[:, :, 0:1], 0.0),
                   w=["XW"], tag="F1 xw0")
                src = x_in.rearrange("(m p) t -> p m t", p=128)[:, :, 0:512]
                dma("sync",
                    XW[:].rearrange("p (m t) -> p m t", t=513)[:, :, 1:513],
                    src, w=["XW"], tag="F1 loadx")
            else:
                src = x_in.rearrange("(m p) t -> p m t", p=128)[
                    :, :, 512 * w_ - 1:512 * w_ + 512]
                dma("sync", XW[:].rearrange("p (m t) -> p m t", t=513), src,
                    w=["XW"], tag="F1 loadx")

        def f1_front_pieces(w_):
            slot = w_ % 3
            rV = f"VRING{slot}"
            rK2 = f"KT2_{slot}"
            rQ2 = f"QT2_{w_ % 2}"
            qt2 = QT2[w_ % 2]
            kt2 = KT2[slot]
            inb = INB[w_ % 2]
            rI = f"INB{w_ % 2}"
            pieces = []

            def ht_piece():
                for mc in range(8):
                    off = 1 if mc < 4 else 0
                    sap = XW[:, mc * 513 + off:mc * 513 + off + 512]
                    op("gpsimd", lambda h, s=sap, mc=mc: h.tensor_copy(
                        HT[:, mc * 512:mc * 512 + 512], s),
                       r=["XW"], w=["HT"], tag="F1 ht")
            pieces.append(ht_piece)

            def qkv_piece(tt, nh):
                qc = (nh if nh < 2 else 3) * 512
                pq = PS[:, qc:qc + 512]
                for mc in range(8):
                    lhs = HT[:, mc * 512 + tt * 128:mc * 512 + tt * 128 + 128]
                    rhs = WQKV[:, mc * 1536 + nh * 512:mc * 1536 + nh * 512 + 512]
                    op("tensor",
                       lambda h, o_=pq, l_=lhs, r_=rhs, st=(mc == 0),
                              sp=(mc == 7):
                       h.matmul(o_, l_, r_, start=st, stop=sp),
                       r=["HT", "WQKV"], w=psb(qc, qc + 512),
                       tag="F1 qkv")
                if nh == 0:
                    op("scalar", lambda h, tt=tt: h.copy(
                        QF[:, tt * 512:tt * 512 + 512], PS[:, 0:512]),
                       r=psb(0, 512), w=["QF"], tag="F1 qev")
                elif nh == 1:
                    op("scalar", lambda h, tt=tt: h.copy(
                        KF[:, tt * 512:tt * 512 + 512], PS[:, 512:1024]),
                       r=psb(512, 1024), w=["KF"], tag="F1 kev")
                else:
                    vd = VRING[slot][:, tt * 520:tt * 520 + 520].rearrange(
                        "p (hh r) -> p hh r", r=65)[:, :, 0:64]
                    vs = PS[:, 1536:2048].rearrange("p (hh d) -> p hh d", d=64)
                    op("scalar", lambda h, d=vd, s2=vs: h.copy(d, s2),
                       r=psb(1536, 2048), w=[rV], tag="F1 vev")
            for tt in range(4):
                for nh in range(3):
                    pieces.append(lambda tt=tt, nh=nh: qkv_piece(tt, nh))

            def vones():
                op("vector", lambda h, s2=slot: h.memset(
                    VRING[s2][:].rearrange("p (b r) -> p b r", r=65)[:, :, 64:65],
                    1.0),
                   w=[rV], tag="F1 vones")
            pieces.append(vones)

            if debug and l == 0 and w_ == 0:
                def dbgs():
                    dbg("HT", HT[:], ["HT"])
                    dbg("QF", QF[:], ["QF"])
                    dbg("KF", KF[:], ["KF"])
                    dbg("VR0", VRING[0][:], ["VRING0"])
                pieces.append(dbgs)

            def norm_piece(which):
                (xf, rxf, col0) = ((QF, "QF", 0), (KF, "KF", 32))[which]
                op("vector", lambda h, x2=xf: h.tensor_tensor(
                    QR[:], x2[:], x2[:], ALU.mult),
                   r=[rxf], w=["QR"], tag="F1 sq")
                op("vector", lambda h, c0=col0: h.reduce_sum(
                    STATC[:, c0:c0 + 32].rearrange("p (g o) -> p g o", o=1),
                    QR[:].rearrange("p (g d) -> p g d", d=64), axis=AX.X),
                   r=["QR"], w=["STATC"], tag="F1 ss")
                op("scalar", lambda h, c0=col0: h.activation(
                    STATC[:, c0:c0 + 32], STATC[:, c0:c0 + 32], AF.Ln),
                   r=["STATC"], w=["STATC"], tag="F1 lnn")
                op("scalar", lambda h, c0=col0: h.activation(
                    STATC[:, c0:c0 + 32], STATC[:, c0:c0 + 32], AF.Exp,
                    scale=0.5),
                   r=["STATC"], w=["STATC"], tag="F1 nrm")
                op("vector", lambda h, c0=col0: h.tensor_scalar(
                    STATC[:, c0:c0 + 32], STATC[:, c0:c0 + 32], 1e-12, None,
                    ALU.max),
                   r=["STATC"], w=["STATC"], tag="F1 clip")
                op("vector", lambda h, c0=col0: h.reciprocal(
                    inb[:, c0:c0 + 32], STATC[:, c0:c0 + 32]),
                   r=["STATC"], w=[rI], tag="F1 inb")
            pieces.append(lambda: norm_piece(0))
            pieces.append(lambda: norm_piece(1))

            def rot_piece(which, part):
                (xf, rxf, T1, rT1, T2, rT2, ur, rur, vr, rvr, hb, rhb, eng) = (
                    (QF, "QF", T1Q, "T1Q", T2Q, "T2Q", QR, "QR", VRA, "VRA",
                     QHB, "QHB", "gpsimd"),
                    (KF, "KF", T1K, "T1K", T2K, "T2K", KR, "KR", VRB, "VRB",
                     KHB, "KHB", "gpsimd"),
                )[which]
                tts = 4 * w_

                def hslice(buf, hh, d0, d1):
                    return buf[:].rearrange("p (t g d) -> p t g d", g=8,
                                            d=64)[:, :, hh, d0:d1]

                t1ap = T1[:, tts * 64:tts * 64 + 256].rearrange(
                    "p (t d) -> p t d", d=64)
                t2lo = T2[:, tts * 64:tts * 64 + 256].rearrange(
                    "p (t d) -> p t d", d=64)[:, :, 0:32]
                t2hi = T2[:, tts * 64:tts * 64 + 256].rearrange(
                    "p (t d) -> p t d", d=64)[:, :, 32:64]
                if part < 4:
                    for hh in range(2 * part, 2 * part + 2):
                        op(eng, lambda h, o_=hslice(ur, hh, 0, 64),
                               a=hslice(xf, hh, 0, 64), b=t1ap:
                           h.tensor_tensor(o_, a, b, ALU.mult),
                           r=[rxf, rT1], w=[rur], tag="F1 rotu")
                        op(eng, lambda h, o_=hslice(vr, hh, 0, 32),
                               a=hslice(xf, hh, 32, 64), b=t2lo:
                           h.tensor_tensor(o_, a, b, ALU.mult),
                           r=[rxf, rT2], w=[rvr], tag="F1 rotvA")
                        op(eng, lambda h, o_=hslice(vr, hh, 32, 64),
                               a=hslice(xf, hh, 0, 32), b=t2hi:
                           h.tensor_tensor(o_, a, b, ALU.mult),
                           r=[rxf, rT2], w=[rvr], tag="F1 rotvB")
                else:
                    op(eng, lambda h, o_=hb, a=ur, b=vr:
                       h.tensor_tensor(o_[:], a[:], b[:], ALU.add),
                       r=[rur, rvr], w=[rhb], tag="F1 rotadd")
                    if which == 1:
                        for g32 in range(32):
                            tt, hh = divmod(g32, 8)
                            sl = KHB[:, tt * 512 + hh * 64:
                                     tt * 512 + hh * 64 + 64]
                            op("vector", lambda h, s2=sl, g32=g32:
                               h.tensor_scalar(
                                   s2, s2, inb[:, 32 + g32:32 + g32 + 1], None,
                                   ALU.mult),
                               r=["KHB", rI], w=["KHB"], tag="F1 kinb")
            for part in range(5):
                pieces.append(lambda part=part: rot_piece(0, part))
            for part in range(5):
                pieces.append(lambda part=part: rot_piece(1, part))

            def tr_piece(which, hp):
                (hb, rhb, dstT, rdst) = ((QHB, "QHB", qt2, rQ2),
                                         (KHB, "KHB", kt2, rK2))[which]
                tb0, tf0, tf1 = ((3072, 1536, 2048) if which == 0
                                 else (2048, 1024, 1536))
                for tt in range(4):
                    pt = PSH[:, tb0 + tt * 128:tb0 + tt * 128 + 128]
                    op("tensor", lambda h, o_=pt, s2=hb, tt=tt, hp=hp:
                       h.transpose(o_, s2[:, tt * 512 + hp * 128:
                                        tt * 512 + hp * 128 + 128], IDBF[:]),
                       r=[rhb, "IDBF"], w=psb(tf0, tf1), tag="F1 tr")
                src = PSH[:, tb0:tb0 + 512]
                op("vector", lambda h, d=dstT, hp=hp, s2=src: h.tensor_copy(
                    d[:, hp * 512:hp * 512 + 512], s2),
                   r=psb(tf0, tf1), w=[rdst], tag="F1 trev")
            for which in range(2):
                for hp in range(4):
                    pieces.append(lambda which=which, hp=hp:
                                  tr_piece(which, hp))
            return pieces

        def f1_attn_blocks(w_):
            qt2 = QT2[w_ % 2]
            rQ2 = f"QT2_{w_ % 2}"
            inb = INB[w_ % 2]
            rI = f"INB{w_ % 2}"

            def f1_attn_S(hp, qt, g):
                gb = 64 * g
                kb_lo = max(0, 4 * w_ + qt - 4)
                kb_hi = 4 * w_ + qt
                nb = kb_hi - kb_lo + 1
                sc0 = 2048 + 1024 * g
                spsum = PS[:, sc0:sc0 + 1024]
                rS = psb(sc0, sc0 + nb * 128)
                lhs = qt2[gb:gb + 64, hp * 512 + qt * 128:
                          hp * 512 + qt * 128 + 128]
                for jj, kb in enumerate(range(kb_lo, kb_hi + 1)):
                    ksl = (kb // 4) % 3
                    rhs = KT2[ksl][gb:gb + 64,
                                   hp * 512 + (kb % 4) * 128:
                                   hp * 512 + (kb % 4) * 128 + 128]
                    o_ = spsum[:, jj * 128:jj * 128 + 128]
                    op("tensor", lambda h, o2=o_, l_=lhs, r_=rhs,
                              sp=(kb != kb_hi):
                       h.matmul(o2, l_, r_, start=True, stop=sp),
                       r=[rQ2, f"KT2_{ksl}"], w=rS, tag="F1 S")
                o_ = spsum[:, (nb - 1) * 128:(nb - 1) * 128 + 128]
                op("tensor", lambda h, o2=o_: h.matmul(
                    o2, IDBF[:], MTRI[:], start=False, stop=True),
                   r=["IDBF", "MTRI"], w=rS, tag="F1 mask")
                hh = 2 * hp + g
                g32 = qt * 8 + hh
                pbuf = PB[g + 2 * (qt % 2)]
                rPB = f"PB{g + 2 * (qt % 2)}"
                op("vector", lambda h, s2=spsum, nb=nb, g=g: h.reduce_max(
                    NMX[:, g:g + 1], s2[:, 0:nb * 128], axis=AX.X,
                    negate=True),
                   r=rS, w=[f"NMXr{g}"], tag="F1 max")
                op("vector", lambda h, g=g, g32=g32: h.tensor_tensor(
                    NMX[:, 2 + g:3 + g], NMX[:, g:g + 1],
                    inb[:, g32:g32 + 1], ALU.mult),
                   r=[f"NMXr{g}", rI], w=[f"NMXs{g}"], tag="F1 mxs")
                op("scalar", lambda h, d=pbuf, s2=spsum, nb=nb, g=g,
                             g32=g32: h.activation(
                    d[:, 0:nb * 128], s2[:, 0:nb * 128], AF.Exp,
                    bias=NMX[:, 2 + g:3 + g],
                    scale=inb[:, g32:g32 + 1]),
                   r=rS + [f"NMXs{g}", rI], w=[rPB], tag="F1 exp")
                if w_ > 0:
                    op("gpsimd", lambda h, d=pbuf: h.affine_select(
                        out=d[:, 0:128], in_=d[:, 0:128],
                        compare_op=ALU.is_ge, fill=0.0, base=0,
                        pattern=[[1, 128]], channel_multiplier=-1),
                       r=[rPB], w=[rPB], tag="F1 asel")

            def f1_attn_tail(hp, qt, g):
                hh = 2 * hp + g
                kb_lo = max(0, 4 * w_ + qt - 4)
                kb_hi = 4 * w_ + qt
                nb = kb_hi - kb_lo + 1
                pbuf = PB[g + 2 * (qt % 2)]
                rPB = f"PB{g + 2 * (qt % 2)}"
                ptb = PTB[g]
                rPT = f"PTB{g}"
                sb0, sf0, sf1 = ((3072, 1536, 2048) if g == 0
                                 else (2048, 1024, 1536))
                for jj in range(nb):
                    pt = PSH[:, sb0 + jj * 128:sb0 + jj * 128 + 128]
                    op("tensor", lambda h, o2=pt, d=pbuf, jj=jj:
                       h.transpose(o2, d[:, jj * 128:jj * 128 + 128],
                                   IDBF[:]),
                       r=[rPB, "IDBF"], w=psb(sf0, sf1), tag="F1 ptr")
                src = PSH[:, sb0:sb0 + nb * 128]
                op("vector", lambda h, d=ptb, s2=src, nb=nb: h.tensor_copy(
                    d[:, 0:nb * 128], s2),
                   r=psb(sf0, sf1), w=[rPT], tag="F1 ptev")
                oc0 = 1920 if g == 0 else 1408
                ops_ = PS[:, oc0:oc0 + 65]
                for jj, kb in enumerate(range(kb_lo, kb_hi + 1)):
                    ksl = (kb // 4) % 3
                    rhs = VRING[ksl][:, (kb % 4) * 520 + hh * 65:
                                     (kb % 4) * 520 + hh * 65 + 65]
                    op("tensor", lambda h, o2=ops_, d=ptb, jj=jj, r_=rhs,
                              st=(jj == 0), sp=(jj == nb - 1):
                       h.matmul(o2, d[:, jj * 128:jj * 128 + 128], r_,
                                start=st, stop=sp),
                       r=[rPT, f"VRING{ksl}"], w=psb(oc0, oc0 + 65),
                       tag="F1 PV")
                op("vector", lambda h, o2=ops_, g=g: h.reciprocal(
                    NMX[:, 4 + g:5 + g], o2[:, 64:65]),
                   r=psb(oc0, oc0 + 65), w=[f"NMXo{g}"], tag="F1 orcp")
                op("scalar", lambda h, o2=ops_, qt=qt, hh=hh, g=g:
                   h.activation(
                       OW[:, qt * 512 + hh * 64:qt * 512 + hh * 64 + 64],
                       o2[:, 0:64], AF.Copy,
                       scale=NMX[:, 4 + g:5 + g]),
                   r=psb(oc0, oc0 + 65) + [f"NMXo{g}"], w=["OW"],
                   tag="F1 onrm")

            blocks = []
            state = {"pend": None}
            for hp in range(4):
                for qt in range(4):
                    def blk(hp=hp, qt=qt):
                        f1_attn_S(hp, qt, 0)
                        f1_attn_S(hp, qt, 1)
                        if state["pend"] is not None:
                            f1_attn_tail(*state["pend"])
                            f1_attn_tail(state["pend"][0], state["pend"][1], 1)
                        state["pend"] = (hp, qt, 0)
                    blocks.append(blk)

            def last():
                f1_attn_tail(*state["pend"])
                f1_attn_tail(state["pend"][0], state["pend"][1], 1)
            blocks.append(last)
            return blocks

        def f1_back(w_):
            if debug and l == 0 and w_ == 0:
                dbg("OW", OW[:], ["OW"])
            for hp in range(4):
                for tt in range(4):
                    pt = PSH[:, 3072 + tt * 128:3072 + tt * 128 + 128]
                    op("tensor", lambda h, o_=pt, tt=tt, hp=hp:
                       h.transpose(o_, OW[:, tt * 512 + hp * 128:
                                         tt * 512 + hp * 128 + 128], IDBF[:]),
                       r=["OW", "IDBF"], w=psb(1536, 2048), tag="F1 otr")
                src = PSH[:, 3072:3584]
                op("vector", lambda h, hp=hp, s2=src: h.tensor_copy(
                    OT[:, hp * 512:hp * 512 + 512], s2),
                   r=psb(1536, 2048), w=["OT"], tag="F1 otev")
            for ct in range(8):
                xr = XR2[ct % 2]
                rXR = f"XR2_{ct % 2}"
                dma("sync", xr[:],
                    x_in[ct * 128:ct * 128 + 128, 512 * w_:512 * w_ + 512],
                    w=[rXR], tag="F1 xr")
                pw = PS[:, 1024:1536]
                for nch in range(4):
                    lhs = WOF[:, nch * 1024 + ct * 128:nch * 1024 + ct * 128 + 128]
                    rhs = OT[:, nch * 512:nch * 512 + 512]
                    op("tensor",
                       lambda h, l_=lhs, r_=rhs, st=(nch == 0), sp=(nch == 3):
                       h.matmul(pw, l_, r_, start=st, stop=sp),
                       r=["OT", "WOF"], w=psb(1024, 1536), tag="F1 wo")
                x2 = X2T[ct % 2]
                rX2 = f"X2T{ct % 2}"
                op("vector", lambda h, ct=ct, x2=x2, xr=xr: h.tensor_tensor(
                    x2[:], PS[:, 1024:1536], xr[:], ALU.add),
                   r=psb(1024, 1536) + [rXR], w=[rX2], tag="F1 res")
                dma("scalar", x_out[ct * 128:ct * 128 + 128,
                                    512 * w_:512 * w_ + 512], x2[:],
                    r=[rX2], w=[f"hbm_f1_{ct}_{w_}"], tag="F1 out")

        f1_load(0)
        for f in f1_front_pieces(0):
            f()
        for w_ in range(8):
            blocks = f1_attn_blocks(w_)
            pieces = []
            if w_ < 7:
                f1_load(w_ + 1)
                pieces = f1_front_pieces(w_ + 1)
            k = 0
            per = (len(pieces) + len(blocks) - 1) // len(blocks)
            for b in blocks:
                b()
                for _ in range(per):
                    if k < len(pieces):
                        pieces[k]()
                        k += 1
            while k < len(pieces):
                pieces[k]()
                k += 1
            f1_back(w_)

    def stage_f2(l, x_in, x_out):
        w13 = ffn_h[l]["w1"].rearrange("(m p) c -> p m c", p=128)
        dma_cast(W1S[:].rearrange("p (m c) -> p m c", c=5460), w13, w=["W1S"],
                 tag="load W1")
        w23 = ffn_h[l]["w2"]
        for jn in range(22):
            r0 = 128 * jn
            r1 = min(2730, r0 + 128)
            dma_cast(W2S[0:r1 - r0, jn * 1024:jn * 1024 + 1024], w23[r0:r1, :],
                     w=["W2S"], tag="load W2")

        def f2_front(c):
            xc2 = XC2[c % 2]
            rX = f"XC2_{c % 2}"
            h2b = H2B[c % 2]
            rH = f"H2B_{c % 2}"
            if c == 0:
                op("vector", lambda h, xc2=xc2: h.memset(
                    xc2[:].rearrange("p (m t) -> p m t", t=257)[:, :, 0:1], 0.0),
                   w=[rX], tag="F2 xc0")
                src = x_in.rearrange("(m p) t -> p m t", p=128)[:, :, 0:256]
                dma("sync",
                    xc2[:].rearrange("p (m t) -> p m t", t=257)[:, :, 1:257],
                    src, w=[rX], tag="F2 loadx")
            else:
                src = x_in.rearrange("(m p) t -> p m t", p=128)[
                    :, :, 256 * c - 1:256 * c + 256]
                dma("sync", xc2[:].rearrange("p (m t) -> p m t", t=257), src,
                    w=[rX], tag="F2 loadx")

            def h2read(mc):
                off = 1 if mc < 4 else 0
                return xc2[:, mc * 257 + off:mc * 257 + off + 256]

            for mc in range(8):
                op("gpsimd", lambda h, mc=mc, s=h2read(mc), h2b=h2b:
                   h.tensor_copy(h2b[:, mc * 256:mc * 256 + 256], s),
                   r=[rX], w=[rH], tag="F2 h2b")

            tagp = f"f2ln{l}_{c}"
            for mc in range(8):
                op("vector", lambda h, mc=mc, h2b=h2b: h.tensor_tensor(
                    R2C[:], h2b[:, mc * 256:mc * 256 + 256],
                    h2b[:, mc * 256:mc * 256 + 256], ALU.mult),
                   r=[rH], w=["R2C"], tag="F2 r2")
                op("tensor", lambda h, mc=mc, h2b=h2b, st=(mc == 0),
                          sp=(mc == 7):
                   h.matmul(PS[0:1, 3072:3328], ONECB[:],
                            h2b[:, mc * 256:mc * 256 + 256], start=st, stop=sp),
                   r=[rH, "ONECB"], w=[tagp + "_s1"] + psb(3072, 3328),
                   tag="F2 s1")
                op("tensor", lambda h, st=(mc == 0), sp=(mc == 7):
                   h.matmul(PS[0:1, 3584:3840], ONECB[:], R2C[:],
                            start=st, stop=sp),
                   r=["R2C", "ONECB"], w=[tagp + "_s2"] + psb(3584, 3840),
                   tag="F2 s2")
            op("vector", lambda h: h.tensor_scalar(
                ROW[0:1, 0:256], PS[0:1, 3072:3328], 1.0 / D, None, ALU.mult),
               r=[tagp + "_s1"] + psb(3072, 3328), w=[tagp + "_mu"], tag="F2 mu")
            op("vector", lambda h: h.tensor_scalar(
                ROW[0:1, 512:768], PS[0:1, 3584:3840], 1.0 / D, None, ALU.mult),
               r=[tagp + "_s2"] + psb(3584, 3840), w=[tagp + "_m2"], tag="F2 m2")
            op("vector", lambda h: h.tensor_tensor(
                ROW[0:1, 1024:1280], ROW[0:1, 0:256], ROW[0:1, 0:256], ALU.mult),
               r=[tagp + "_mu"], w=[tagp + "_v"], tag="F2 mu2")
            op("vector", lambda h: h.tensor_tensor(
                ROW[0:1, 1024:1280], ROW[0:1, 512:768], ROW[0:1, 1024:1280],
                ALU.subtract),
               r=[tagp + "_m2", tagp + "_v"], w=[tagp + "_v"], tag="F2 var")
            op("scalar", lambda h: h.activation(
                ROW[0:1, 1024:1280], ROW[0:1, 1024:1280], AF.Ln, bias=EPS),
               r=[tagp + "_v", "CEPS"], w=[tagp + "_v"], tag="F2 ln")
            op("scalar", lambda h: h.activation(
                ROW[0:1, 1024:1280], ROW[0:1, 1024:1280], AF.Exp, scale=-0.5),
               r=[tagp + "_v"], w=[tagp + "_v"], tag="F2 rstd")
            op("tensor", lambda h: h.matmul(
                PS[:, 3072:3328], ONESF[0:1, 0:128], ROW[0:1, 0:256],
                start=True, stop=True),
               r=[tagp + "_mu"], w=[tagp + "_b1"] + psb(3072, 3328),
               tag="F2 bc1")
            op("vector", lambda h: h.tensor_copy(MUB2[:], PS[:, 3072:3328]),
               r=[tagp + "_b1"] + psb(3072, 3328), w=[tagp + "_mub"], tag="F2")
            op("tensor", lambda h: h.matmul(
                PS[:, 3584:3840], ONESF[0:1, 0:128], ROW[0:1, 1024:1280],
                start=True, stop=True),
               r=[tagp + "_v"], w=[tagp + "_b2"] + psb(3584, 3840),
               tag="F2 bc2")
            op("vector", lambda h: h.tensor_copy(RSB2[:], PS[:, 3584:3840]),
               r=[tagp + "_b2"] + psb(3584, 3840), w=[tagp + "_rsb"], tag="F2")
            for mc in range(8):
                op("vector", lambda h, mc=mc, s=h2read(mc): h.tensor_tensor(
                    GTMP[:, 0:256], s, MUB2[:], ALU.subtract),
                   r=[rX, tagp + "_mub"], w=["GTMP0"], tag="F2 ap1")
                op("vector", lambda h: h.tensor_tensor(
                    GTMP[:, 0:256], GTMP[:, 0:256], RSB2[:], ALU.mult),
                   r=["GTMP0", tagp + "_rsb"], w=["GTMP0"], tag="F2 ap2")
                op("vector", lambda h, mc=mc, l=l, h2b=h2b: h.tensor_scalar(
                    h2b[:, mc * 256:mc * 256 + 256], GTMP[:, 0:256],
                    SM[l]["lngT"][:, mc:mc + 1], SM[l]["lnbT"][:, mc:mc + 1],
                    ALU.mult, ALU.add),
                   r=["GTMP0", f"SM{l}lngT", f"SM{l}lnbT"], w=[rH],
                   tag="F2 ap3")

        def f2_w1(c):
            h2b = H2B[c % 2]
            rH = f"H2B_{c % 2}"
            for j in range(22):
                M = 128 if j < 21 else 42
                ca = 128 * j
                cg = 2730 + 128 * j
                pa = PS[:, 0 + 512 * (j % 2):256 + 512 * (j % 2)]
                pg = PS[:, 1024 + 512 * (j % 2):1280 + 512 * (j % 2)]
                for mc in range(8):
                    lhsa = W1S[:, mc * 5460 + ca:mc * 5460 + ca + M]
                    lhsg = W1S[:, mc * 5460 + cg:mc * 5460 + cg + M]
                    rhs = h2b[:, mc * 256:mc * 256 + 256]
                    op("tensor", lambda h, pa=pa, l_=lhsa, r_=rhs, M=M,
                              st=(mc == 0), sp=(mc == 7):
                       h.matmul(pa[0:M, :], l_, r_, start=st, stop=sp),
                       r=[rH, "W1S"], w=psb(512 * (j % 2), 512 * (j % 2) + 256),
                       tag="F2 w1a")
                    op("tensor", lambda h, pg=pg, l_=lhsg, r_=rhs, M=M,
                              st=(mc == 0), sp=(mc == 7):
                       h.matmul(pg[0:M, :], l_, r_, start=st, stop=sp),
                       r=[rH, "W1S"],
                       w=psb(1024 + 512 * (j % 2), 1024 + 512 * (j % 2) + 256),
                       tag="F2 w1g")
                op("scalar", lambda h, M=M, j=j, l=l, pg=pg: h.activation(
                    GTMP[0:M, 256 * (j % 2):256 * (j % 2) + 256], pg[0:M, :],
                    AF.Gelu,
                    bias=SM[l]["b1g"][0:M, j:j + 1]),
                   r=psb(1024 + 512 * (j % 2), 1024 + 512 * (j % 2) + 256)
                   + [f"SM{l}b1g"], w=[f"GTMP{j % 2}"], tag="F2 gelu")
                op("vector", lambda h, M=M, j=j, l=l, pa=pa:
                   h.scalar_tensor_tensor(
                    ACTB[0:M, j * 256:j * 256 + 256], pa[0:M, :],
                    SM[l]["b1a"][0:M, j:j + 1],
                    GTMP[0:M, 256 * (j % 2):256 * (j % 2) + 256],
                    ALU.add, ALU.mult),
                   r=psb(512 * (j % 2), 512 * (j % 2) + 256)
                   + [f"GTMP{j % 2}", f"SM{l}b1a"], w=["ACTB"],
                   tag="F2 act")

        def f2_w2(c):
            xc2 = XC2[c % 2]
            rX = f"XC2_{c % 2}"
            for ct in range(8):
                pwc0 = 2048 + 512 * (ct % 2)
                pw = PS[:, pwc0:pwc0 + 256]
                for jn in range(22):
                    M = 128 if jn < 21 else 42
                    lhs = W2S[0:M, jn * 1024 + ct * 128:jn * 1024 + ct * 128 + 128]
                    rhs = ACTB[0:M, jn * 256:jn * 256 + 256]
                    op("tensor", lambda h, pw=pw, l_=lhs, r_=rhs, st=(jn == 0),
                              sp=(jn == 21):
                       h.matmul(pw, l_, r_, start=st, stop=sp),
                       r=["ACTB", "W2S"], w=psb(pwc0, pwc0 + 256), tag="F2 w2")
                x3 = X3T[ct % 2]
                rX3 = f"X3T{ct % 2}"
                op("vector", lambda h, ct=ct, x3=x3, l=l, pw=pw, xc2=xc2:
                   h.scalar_tensor_tensor(
                    x3[:], pw, SM[l]["b2T"][:, ct:ct + 1],
                    xc2[:, ct * 257 + 1:ct * 257 + 257], ALU.add, ALU.add),
                   r=psb(pwc0, pwc0 + 256) + [f"SM{l}b2T", rX], w=[rX3],
                   tag="F2 res")
                dma("scalar", x_out[ct * 128:ct * 128 + 128,
                                    256 * c:256 * c + 256], x3[:],
                    r=[rX3], w=[f"hbm_f2_{ct}_{c}"], tag="F2 out")

        f2_front(0)
        for c in range(16):
            f2_w1(c)
            if c < 15:
                f2_front(c + 1)
            f2_w2(c)

    P.fence()
    stage_f1(0, xS["xA"], xS["xB"])
    P.fence()
    stage_f2(0, xS["xB"], xS["xC"])
    P.fence()
    stage_f1(1, xS["xC"], xS["xD"])
    P.fence()
    stage_f2(1, xS["xD"], xS["xE"])
    P.fence()

    for c in range(8):
        s1 = xS["xA"].rearrange("(m p) t -> p m t", p=128)[
            :, :, 512 * c:512 * c + 512]
        s2 = xS["xE"].rearrange("(m p) t -> p m t", p=128)[
            :, :, 512 * c:512 * c + 512]
        xf1, rF1 = XF1[c % 2], f"XF1_{c % 2}"
        xf2, rF2 = XF2[c % 2], f"XF2_{c % 2}"
        dma("sync", xf1[:].rearrange("p (m t) -> p m t", t=512), s1,
            w=[rF1], tag="FL load1")
        dma("gpsimd", xf2[:].rearrange("p (m t) -> p m t", t=512), s2,
            w=[rF2], tag="FL load2")
        op("vector", lambda h, xf1=xf1, xf2=xf2: h.tensor_tensor(
            xf1[:], xf1[:], xf2[:], ALU.add),
           r=[rF1, rF2], w=[rF1], tag="FL add")

        def fl_apply(mc, rmub, rrsb, c=c, xf1=xf1, rF1=rF1):
            ot = OUTT[mc % 2]
            rO = f"OUTT{mc % 2}"
            op("vector", lambda h, mc=mc, ot=ot, xf1=xf1: h.tensor_tensor(
                ot[:], xf1[:, mc * 512:mc * 512 + 512], MUB[:], ALU.subtract),
               r=[rF1, rmub], w=[rO], tag="FL ap1")
            op("vector", lambda h, ot=ot: h.tensor_tensor(
                ot[:], ot[:], RSB[:], ALU.mult),
               r=[rO, rrsb], w=[rO], tag="FL ap2")
            dma("scalar", out_h[mc * 128:mc * 128 + 128, 512 * c:512 * c + 512],
                ot[:], r=[rO], w=[f"hbm_out_{mc}_{c}"], tag="FL out")

        ln_channel(512,
                   lambda mc, xf1=xf1: xf1[:, mc * 512:mc * 512 + 512], [rF1],
                   RBT, R2T, MUB, RSB, fl_apply, tagp=f"flln{c}")

    stats = P.emit()
    return nc, stats


def _host_consts():
    DIM_HEAD, WINDOW = 64, 512
    half = np.arange(0, DIM_HEAD, 2, dtype=np.float64)
    inv_freq = 1.0 / 10000.0 ** (half / DIM_HEAD)
    t = np.arange(L, dtype=np.float64)
    freqs = t[:, None] * inv_freq[None, :]
    freqs = np.concatenate([freqs, freqs], -1)
    base = (half + 0.4 * DIM_HEAD) / (1.4 * DIM_HEAD)
    power = (t - L // 2) / float(WINDOW // 2)
    scale = base[None, :] ** power[:, None]
    scale = np.concatenate([scale, scale], -1)
    cos, sin = np.cos(freqs), np.sin(freqs)

    def arr(x):
        return np.ascontiguousarray(
            x.astype(np.float32).reshape(32, 128, 64).transpose(1, 0, 2)
            .reshape(128, 2048))

    tabs = dict(cs=arr(cos * scale), ss=arr(sin * scale),
                ci=arr(cos / scale), si=arr(sin / scale))
    identf = np.eye(128, dtype=np.float32)
    mtri = np.triu(np.ones((128, 128), np.float32), k=1) * np.float32(NEG)
    return tabs, identf, mtri


def _colmajor8(v):
    return np.ascontiguousarray(v.reshape(8, 128).T.astype(np.float32))


def _b1cols(v):
    w = np.zeros(22 * 128, np.float32)
    w[:2730] = v
    return np.ascontiguousarray(w.reshape(22, 128).T)


def kernel(L0_attn_x, L0_attn_Wq, L0_attn_bq, L0_attn_Wk, L0_attn_bk,
           L0_attn_Wv, L0_attn_bv, L0_attn_Wo, L0_attn_bo,
           L0_ffn_wqkv, L0_ffn_q_scale, L0_ffn_k_scale, L0_ffn_wo,
           L0_ffn_ln_g, L0_ffn_ln_b, L0_ffn_w1, L0_ffn_b1, L0_ffn_w2,
           L0_ffn_b2):
    f32 = np.float32
    x = np.asarray(L0_attn_x, f32)
    tabs, identf, mtri = _host_consts()
    common = dict(
        wq=np.ascontiguousarray(np.asarray(L0_attn_Wq, f32)),
        wk=np.ascontiguousarray(np.asarray(L0_attn_Wk, f32)),
        wv=np.ascontiguousarray(np.asarray(L0_attn_Wv, f32)),
        woa=np.ascontiguousarray(np.asarray(L0_attn_Wo, f32)),
        bq=np.asarray(L0_attn_bq, f32).reshape(1, D),
        bk=np.asarray(L0_attn_bk, f32).reshape(1, D),
        bvT=_colmajor8(np.asarray(L0_attn_bv, f32)),
        boT=_colmajor8(np.asarray(L0_attn_bo, f32)),
        identf=identf, mtri=mtri,
    )
    for t in ("cs", "ss", "ci", "si"):
        common[f"tab_{t}"] = tabs[t]
    for l in (0, 1):
        common[f"wqkv{l}"] = np.ascontiguousarray(np.asarray(L0_ffn_wqkv[l], f32))
        common[f"wof{l}"] = np.ascontiguousarray(np.asarray(L0_ffn_wo[l], f32))
        common[f"w1_{l}"] = np.ascontiguousarray(np.asarray(L0_ffn_w1[l], f32))
        common[f"w2_{l}"] = np.ascontiguousarray(np.asarray(L0_ffn_w2[l], f32))
        b1 = np.asarray(L0_ffn_b1[l], f32)
        common[f"b1a{l}"] = _b1cols(b1[:2730])
        common[f"b1g{l}"] = _b1cols(b1[2730:])
        common[f"b2T{l}"] = _colmajor8(np.asarray(L0_ffn_b2[l], f32))
        common[f"lngT{l}"] = _colmajor8(np.asarray(L0_ffn_ln_g[l], f32))
        common[f"lnbT{l}"] = _colmajor8(np.asarray(L0_ffn_ln_b[l], f32))
        common[f"qsrep{l}"] = np.ascontiguousarray(
            np.tile(np.asarray(L0_ffn_q_scale[l], f32), 32).reshape(1, 2048))
        common[f"ksrep{l}"] = np.ascontiguousarray(
            np.tile(np.asarray(L0_ffn_k_scale[l], f32), 32).reshape(1, 2048))

    nc, _ = build_nc(debug=False)
    in_maps = []
    for b in range(NCORE):
        m = dict(common)
        m["xT"] = np.ascontiguousarray(x[b].T)
        in_maps.append(m)
    res = run_bass_kernel_spmd(nc, in_maps, core_ids=list(range(NCORE)))
    out = np.stack([np.ascontiguousarray(r["out"].T) for r in res.results])
    return out
